# version 14
# baseline (speedup 1.0000x reference)
import numpy as np
import ml_dtypes
import concourse.bass as bass
import concourse.mybir as mybir
from concourse.bass_utils import run_bass_kernel_spmd
from concourse.alu_op_type import AluOpType
import concourse.bass_utils as _bu

if not getattr(_bu, "_ldw_patch", False):
    _bu._ldw_patch = True
    _orig_run_command = _bu.run_command

    def _run_command_ldw(argv, **kw):
        argv = ["--enable-ldw-opt=true" if a == "--enable-ldw-opt=false" else a
                for a in argv]
        return _orig_run_command(argv, **kw)

    _bu.run_command = _run_command_ldw

F32 = mybir.dt.float32
BF16 = mybir.dt.bfloat16
I32 = mybir.dt.int32
AF = mybir.ActivationFunctionType
AX = mybir.AxisListType

P = 128
E = 4096
ECH = 32
DFF = 11008
DFFC = 86
THIRDS = (29, 29, 28)
THMAX = 29
FCQKV = 48
TPC = 1024
NCH = 8
NQH = 32
NKVH = 8
HD = 128
SCALE = 1.0 / float(np.sqrt(128.0))
EPS = 1e-5
NEG = -1e30

FCS = []
for _kv in range(NKVH):
    FCS.append(32 + _kv)
    FCS.append(40 + _kv)
FCS += list(range(32))

ENGINES = ("pe", "ve", "sc", "sp", "gp")


class Plan:
    def __init__(self):
        self.ops = {e: [] for e in ENGINES}
        self.cnt = {}

    def sem(self, *names):
        for n in names:
            self.cnt[n] = 0

    def op(self, eng, fn, waits=(), inc=None, amt=1, drain=False):
        self.ops[eng].append((tuple(waits), fn, inc, amt, drain))
        if inc is not None:
            self.cnt[inc] += amt
            return self.cnt[inc]
        return None

    def at(self, name):
        return self.cnt[name]


def build_nc(debug=False):
    nc = bass.Bass("TRN2", target_bir_lowering=False)

    d_x = nc.dram_tensor("x", [TPC, E], F32, kind="ExternalInput").ap()
    d_win = nc.dram_tensor("w_in_t", [FCQKV, ECH, P, P], F32, kind="ExternalInput").ap()
    d_wout = nc.dram_tensor("w_out_t", [32, 32, P, P], F32, kind="ExternalInput").ap()
    d_wg = nc.dram_tensor("gate_t", [DFFC, ECH, P, P], I32, kind="ExternalInput").ap()
    d_wu = nc.dram_tensor("up_t", [DFFC, ECH, P, P], I32, kind="ExternalInput").ap()
    d_wd = nc.dram_tensor("down_t", [32, DFFC, P, P], I32, kind="ExternalInput").ap()
    d_gsc = nc.dram_tensor("gsc", [P, DFFC], F32, kind="ExternalInput").ap()
    d_usc = nc.dram_tensor("usc", [P, DFFC], F32, kind="ExternalInput").ap()
    d_dsc = nc.dram_tensor("dsc", [P, 32], F32, kind="ExternalInput").ap()
    d_cos = nc.dram_tensor("costab", [P, TPC], BF16, kind="ExternalInput").ap()
    d_sin = nc.dram_tensor("sintab", [P, TPC], BF16, kind="ExternalInput").ap()
    d_mask = nc.dram_tensor("maskt", [P, 512], F32, kind="ExternalInput").ap()
    d_idb = nc.dram_tensor("identb", [P, P], BF16, kind="ExternalInput").ap()
    d_negb = nc.dram_tensor("negb", [P, 1], F32, kind="ExternalInput").ap()
    d_ones = nc.dram_tensor("onest", [P, P], BF16, kind="ExternalInput").ap()
    d_mkt = nc.dram_tensor("masktt", [P, 2048], F32, kind="ExternalInput").ap()
    d_idf = nc.dram_tensor("identf", [P, P], F32, kind="ExternalInput").ap()
    d_out = nc.dram_tensor("out", [TPC, E], F32, kind="ExternalOutput").ap()

    if debug:
        d_dbg_lt = nc.dram_tensor("dbg_lt", [P, 16], F32, kind="ExternalOutput").ap()
        d_dbg_xn = nc.dram_tensor("dbg_xn", [P, 2048], BF16, kind="ExternalOutput").ap()
        d_dbg_q = nc.dram_tensor("dbg_q", [P, TPC], BF16, kind="ExternalOutput").ap()
        d_dbg_k = nc.dram_tensor("dbg_k", [P, TPC], BF16, kind="ExternalOutput").ap()
        d_dbg_v = nc.dram_tensor("dbg_v", [P, TPC], BF16, kind="ExternalOutput").ap()
        d_dbg_p = nc.dram_tensor("dbg_p", [P, 512], BF16, kind="ExternalOutput").ap()
        d_dbg_a = nc.dram_tensor("dbg_a", [P, TPC], BF16, kind="ExternalOutput").ap()
        d_dbg_p0 = nc.dram_tensor("dbg_p0", [P, TPC], F32, kind="ExternalOutput").ap()
        d_dbg_h = nc.dram_tensor("dbg_h", [P, TPC], BF16, kind="ExternalOutput").ap()
    d_xn = nc.dram_tensor("xn_spill", [P, ECH * TPC], BF16, kind="Internal").ap()
    d_attn = nc.dram_tensor("attn_spill", [NKVH, P, 4 * TPC], BF16, kind="Internal").ap()
    d_p0 = nc.dram_tensor("p0", [32, P, TPC], F32, kind="Internal").ap()
    d_p1 = nc.dram_tensor("p1", [32, P, TPC], F32, kind="Internal").ap()
    d_p2 = nc.dram_tensor("p2", [32, P, TPC], F32, kind="Internal").ap()
    d_p3 = nc.dram_tensor("p3", [32, P, TPC], F32, kind="Internal").ap()
    d_kvin = [nc.dram_tensor(f"kvin{k}", [2048, P], BF16, kind="Internal").ap()
              for k in range(NKVH)]
    d_kvout = [nc.dram_tensor(f"kvout{k}", [8192, P], BF16, kind="Internal").ap()
               for k in range(NKVH)]

    pl = Plan()
    sems_all = (["LDC", "LDBG", "LSP", "LXR", "REL", "CC", "PE", "VE", "SC"]
                + [f"LX{i}" for i in range(2)]
                + [f"LST{i}" for i in range(8)]
                + [f"ASM{i}" for i in range(2)]
                + [f"ASP{i}" for i in range(2)]
                + [f"WQ{i}" for i in range(2)]
                + [f"WO{i}" for i in range(2)]
                + [f"WG{i}" for i in range(2)]
                + [f"WU{i}" for i in range(2)]
                + [f"WN{i}" for i in range(2)]
                + [f"PS{i}" for i in range(2)]
                + [f"P0{i}" for i in range(2)]
                + [f"PN0{i}" for i in range(2)]
                + [f"PN1{i}" for i in range(2)]
                + [f"PN2{i}" for i in range(2)]
                + [f"FL{i}" for i in range(2)]
                + [f"X2{i}" for i in range(2)]
                + [f"ST{i}" for i in range(2)])
    pl.sem(*sems_all)

    open_cms = {}

    def alloc(name, shape, dt, side="right"):
        cm = nc.sbuf_tensor(name, shape, dt, side=side)
        h = cm.__enter__()
        open_cms[name] = cm
        return h

    def palloc(name, shape, dt):
        cm = nc.psum_tensor(name, shape, dt)
        h = cm.__enter__()
        open_cms[name] = cm
        return h

    def free(*names):
        for n in names:
            open_cms.pop(n).__exit__(None, None, None)

    snap = {"PE": 0, "VE": 0, "SC": 0}

    def gate():
        return [("PE", snap["PE"]), ("VE", snap["VE"]), ("SC", snap["SC"])]

    def endphase():
        snap["PE"] = pl.at("PE")
        snap["VE"] = pl.at("VE")
        snap["SC"] = pl.at("SC")

    c_cos = alloc("c_cos", [P, TPC], BF16, side="left")
    c_sin = alloc("c_sin", [P, TPC], BF16, side="left")
    c_mask = alloc("c_mask", [P, 512], F32, side="left")
    c_idb = alloc("c_idb", [P, P], BF16, side="left")
    c_nb = alloc("c_nb", [P, 1], F32, side="left")
    c_ones = alloc("c_ones", [P, P], BF16, side="left")
    c_mkT = alloc("c_mkT", [P, 2048], F32, side="left")
    c_idf = alloc("c_idf", [P, P], F32, side="left")
    c_gsc = alloc("c_gsc", [P, DFFC], F32, side="left")
    c_usc = alloc("c_usc", [P, DFFC], F32, side="left")
    c_dsc = alloc("c_dsc", [P, 32], F32, side="left")
    for dst, src in [(c_cos, d_cos), (c_sin, d_sin), (c_mask, d_mask),
                     (c_idb, d_idb), (c_nb, d_negb), (c_ones, d_ones),
                     (c_mkT, d_mkt), (c_idf, d_idf), (c_gsc, d_gsc),
                     (c_usc, d_usc), (c_dsc, d_dsc)]:
        pl.op("sp", lambda s, a=dst, b=src: s.dma_start(out=a[:], in_=b[:]),
              inc="LDC", amt=16)

    xn_fm = alloc("xn_fm", [P, ECH * TPC], BF16)
    xt = alloc("xt", [P, 2 * E], F32)
    xn_tm = alloc("xn_tm", [P, 2 * E], BF16)
    lt = alloc("lt", [P, 16], F32)
    p_tr0 = palloc("p_tr0", [P, 1024], BF16)
    p_tr1 = palloc("p_tr1", [P, 1024], BF16)
    p_trs = (p_tr0, p_tr1)
    xnfm3 = xn_fm[:].rearrange("p (ec t) -> p ec t", ec=ECH)

    cp_lnxn = {}
    cp_ev0 = {}
    cp_trpe = {}
    for tc in range(NCH):
        par = tc % 2
        xts = xt[:, par * E:(par + 1) * E]
        xnts = xn_tm[:, par * E:(par + 1) * E]
        b = 8 * par
        C = lambda i, bb=b: lt[:, bb + i:bb + i + 1]
        lxs = f"LX{tc % 2}"
        w = [("VE", cp_lnxn[tc - 2])] if tc >= 2 else []
        pl.op("sp", lambda s, a=xts, t=tc: s.dma_start(
            out=a, in_=d_x[t * P:(t + 1) * P, :]), waits=w, inc=lxs, amt=16)
        ldx = (lxs, pl.at(lxs))
        w = [ldx] + ([("VE", cp_lnxn[tc - 2])] if tc >= 2 else [("LDC", pl.at("LDC"))])
        pl.op("ve", lambda v, a=xts, c=C(0): v.reduce_sum(c, a, axis=AX.X),
              waits=w, inc="VE", drain=True)
        w = [ldx] + ([("PE", cp_trpe[(tc - 2, 3)])] if tc >= 2 else [("LDC", pl.at("LDC"))])
        pl.op("sc", lambda s, a=xts, o=xnts, c=C(1): s.activation(
            out=o, in_=a, func=AF.Square, accum_out=c), waits=w, inc="SC")
        sc_sq = pl.at("SC")
        pl.op("ve", lambda v, c0=C(0), c2=C(2): v.tensor_scalar_mul(c2, c0, 1.0 / E),
              inc="VE", drain=True)
        pl.op("ve", lambda v, c1=C(1), c3=C(3): v.tensor_scalar_mul(c3, c1, 1.0 / E),
              waits=[("SC", sc_sq)], inc="VE", drain=True)
        ve_m = pl.at("VE")
        pl.op("sc", lambda s, c2=C(2), c4=C(4): s.activation(out=c4, in_=c2, func=AF.Square),
              waits=[("VE", ve_m)], inc="SC")
        sc_m2 = pl.at("SC")
        pl.op("ve", lambda v, c3=C(3), c4=C(4), c5=C(5): v.tensor_sub(c5, c3, c4),
              waits=[("SC", sc_m2)], inc="VE", drain=True)
        ve_var = pl.at("VE")
        pl.op("ve", lambda v, c5=C(5): v.tensor_scalar_add(c5, c5, EPS), inc="VE",
              drain=True)
        ve_var2 = pl.at("VE")
        pl.op("sc", lambda s, c5=C(5), c6=C(6): s.activation(
            out=c6, in_=c5, func=AF.Ln), waits=[("VE", ve_var2)], inc="SC",
              drain=True)
        pl.op("sc", lambda s, c6=C(6): s.activation(out=c6, in_=c6, func=AF.Exp, scale=-0.5),
              inc="SC")
        sc_rstd = pl.at("SC")
        pl.op("ve", lambda v, c2=C(2), c6=C(6), c7=C(7): v.scalar_tensor_tensor(
            c7, c2, -1.0, c6, AluOpType.mult, AluOpType.mult),
              waits=[("SC", sc_rstd)], inc="VE", drain=True)
        ve_nmr = pl.at("VE")
        pl.op("ve", lambda v, a=xts, o=xnts, c6=C(6), c7=C(7): v.tensor_scalar(
            o, a, c6, c7, AluOpType.mult, AluOpType.add),
              waits=[("VE", ve_nmr)], inc="VE")
        cp_lnxn[tc] = pl.at("VE")

        for g in range(4):
            gpar = g % 2
            pp = p_trs[gpar]
            w = [("VE", cp_lnxn[tc])]
            if tc == 0 and g == 0:
                w.append(("LDC", pl.at("LDC")))
            prev = (tc, g - 2) if g >= 2 else ((tc - 1, g + 2) if tc >= 1 else None)
            if prev is not None:
                w.append(("VE", cp_ev0[prev]))

            def tr_fn(t, xn=xnts, pp=pp, g=g):
                last = None
                for i in range(8):
                    ec = 8 * g + i
                    last = t.transpose(pp[:, i * P:(i + 1) * P],
                                       xn[:, ec * P:(ec + 1) * P], c_idb[:])
                return last
            pl.op("pe", tr_fn, waits=w, inc="PE")
            cp_trpe[(tc, g)] = pl.at("PE")
            pl.op("ve", lambda v, pp=pp, g=g, tc=tc: v.tensor_copy(
                xnfm3[:, 8 * g:8 * g + 8, tc * P:(tc + 1) * P], pp[:]),
                  waits=[("PE", cp_trpe[(tc, g)])], inc="VE")
            cp_ev0[(tc, g)] = pl.at("VE")
    cp_xnfm_done = pl.at("VE")
    if debug:
        pl.op("sp", lambda s: s.dma_start(out=d_dbg_lt[:], in_=lt[:]),
              waits=[("VE", cp_lnxn[0])], inc="LDBG", amt=16)
        pl.op("sp", lambda s: s.dma_start(out=d_dbg_xn[:], in_=xn_fm[:, 0:2048]),
              waits=[("VE", cp_xnfm_done)], inc="LDBG", amt=16)
    endphase()

    free("lt", "xn_tm", "xt")
    wqkv = alloc("wqkv", [P, 2 * ECH * P], BF16)
    q_fm = alloc("q_fm", [P, NQH * TPC], BF16, side="left")
    k_own = alloc("k_own", [P, NKVH * TPC], BF16)
    v_own = alloc("v_own", [P, NKVH * TPC], BF16)
    r_t1 = alloc("r_t1", [P, TPC], BF16)
    r_t2 = alloc("r_t2", [P, TPC], BF16)
    vtmp = alloc("vtmp", [P, TPC], BF16)
    p_qk0 = palloc("p_qk0", [P, 1024], F32)
    p_qk1 = palloc("p_qk1", [P, 1024], F32)
    p_qks = (p_qk0, p_qk1)

    pl.op("sp", lambda s: s.dma_start(out=d_xn[:], in_=xn_fm[:]),
          waits=[("VE", cp_xnfm_done)], inc="LSP", amt=16)
    cp_xn_spill = ("LSP", pl.at("LSP"))

    d_win4 = d_win.rearrange("fc ec e f -> fc e ec f")
    xnv = xnfm3

    cp_qkv_mm = {}
    cp_qkv_ev = {}
    cp_stage = {}
    cp_cc = {}
    v_evac = {}
    vtr_prev = None
    for i, fc in enumerate(FCS):
        par = i % 2
        wsl = wqkv[:, par * ECH * P:(par + 1) * ECH * P]
        w = [("PE", cp_qkv_mm[FCS[i - 2]])] if i >= 2 else gate()
        wqs = f"WQ{i % 2}"
        pl.op("gp", lambda g, a=wsl, fc=fc: g.dma_start(
            out=a.rearrange("p (n ec f) -> p n ec f", n=1, ec=ECH),
            in_=d_win4[fc:fc + 1, :, :, :].rearrange("fc e ec f -> e fc ec f")),
              waits=w, inc=wqs, amt=16)
        wd = (wqs, pl.at(wqs))
        pp = p_qks[par]
        w0 = [wd]
        if i == 0:
            w0.append(("VE", cp_ev0[(3, 3)]))
        if i >= 2:
            w0.append(("VE", cp_qkv_ev[FCS[i - 2]]))

        def qkv_half(t, wsl=wsl, pp=pp, tg=0):
            last = None
            for ec in range(ECH):
                last = t.matmul(pp[:, tg * 512:(tg + 1) * 512],
                                wsl[:, ec * P:(ec + 1) * P],
                                xnv[:, ec, tg * 512:(tg + 1) * 512],
                                start=(ec == 0), stop=(ec == ECH - 1))
            return last
        pl.op("pe", lambda t, f=qkv_half: f(t, tg=0), waits=w0, inc="PE")
        w1 = [("VE", cp_xnfm_done)] if i == 0 else []
        pl.op("pe", lambda t, f=qkv_half: f(t, tg=1), waits=w1, inc="PE")
        cp_qkv_mm[fc] = pl.at("PE")

        if fc < 40:
            if fc < 32:
                kvi_, hpi_ = fc // 4, fc % 4
                q5 = q_fm[:].rearrange("p (kv qc hp q) -> p kv qc hp q",
                                       kv=8, qc=8, hp=4, q=P)
                dest = q5[:, kvi_, :, hpi_, :]
            else:
                dest = k_own[:, (fc - 32) * TPC:(fc - 31) * TPC].rearrange(
                    "p (qc q) -> p qc q", qc=8)
            pl.op("ve", lambda v, pp=pp: v.tensor_mul(r_t1[:], pp[:], c_cos[:]),
                  waits=[("PE", cp_qkv_mm[fc])], inc="VE")
            pl.op("ve", lambda v, pp=pp: v.tensor_mul(
                r_t2[0:64, :], pp[64:128, :], c_sin[0:64, :]), inc="VE")
            pl.op("ve", lambda v, pp=pp: v.tensor_mul(
                r_t2[64:128, :], pp[0:64, :], c_sin[64:128, :]), inc="VE",
                  drain=True)
            pl.op("ve", lambda v, d=dest: v.tensor_add(
                d, r_t1[:].rearrange("p (a q) -> p a q", q=P),
                r_t2[:].rearrange("p (a q) -> p a q", q=P)), inc="VE")
            cp_qkv_ev[fc] = pl.at("VE")
        else:
            kv = fc - 40
            w = [("PE", cp_qkv_mm[fc])]
            if vtr_prev is not None:
                w.append(("PE", vtr_prev))
            pl.op("ve", lambda v, pp=pp: v.tensor_copy(vtmp[:], pp[:]),
                  waits=w, inc="VE")
            cp_qkv_ev[fc] = pl.at("VE")
            vp = p_trs[kv % 2]
            w = [("VE", cp_qkv_ev[fc])]
            if kv >= 2:
                w.append(("VE", v_evac[kv - 2]))
            elif kv == 0:
                w.append(("VE", cp_xnfm_done))

            def vtr_fn(t, vp=vp):
                last = None
                for tcc in range(8):
                    last = t.transpose(vp[:, tcc * P:(tcc + 1) * P],
                                       vtmp[:, tcc * P:(tcc + 1) * P], c_idb[:])
                return last
            pl.op("pe", vtr_fn, waits=w, inc="PE")
            vtr_prev = pl.at("PE")
            pl.op("ve", lambda v, vp=vp, kv=kv: v.tensor_copy(
                v_own[:, kv * TPC:(kv + 1) * TPC], vp[:]),
                  waits=[("PE", vtr_prev)], inc="VE")
            v_evac[kv] = pl.at("VE")

            kvi = d_kvin[kv]
            k_view = kvi.rearrange("(v2 d e) c -> d v2 (e c)", v2=2, d=P, e=8)
            v_view = kvi.rearrange("(v2 tc p) c -> p v2 tc c", v2=2, tc=8, p=P)
            pl.op("sp", lambda s, kv=kv, k_view=k_view: s.dma_start(
                out=k_view[:, 0:1, :],
                in_=k_own[:, kv * TPC:(kv + 1) * TPC].rearrange(
                    "p (n t) -> p n t", n=1)),
                  waits=[("VE", cp_qkv_ev[32 + kv])], inc=f"LST{kv}", amt=16)
            pl.op("sp", lambda s, kv=kv, v_view=v_view: s.dma_start(
                out=v_view[:, 1:2, :, :],
                in_=v_own[:, kv * TPC:(kv + 1) * TPC].rearrange(
                    "p (n tc c) -> p n tc c", n=1, tc=8)),
                  waits=[("VE", v_evac[kv])], inc=f"LST{kv}", amt=16)
            cp_stage[kv] = (f"LST{kv}", pl.at(f"LST{kv}"))
            pl.op("gp", lambda g, kv=kv: g.collective_compute(
                "AllGather", AluOpType.bypass,
                replica_groups=[[0, 1, 2, 3], [4, 5, 6, 7]],
                ins=[d_kvin[kv][:]], outs=[d_kvout[kv][:]]),
                  waits=[cp_stage[kv], ("CC", kv)], inc="CC", amt=1)
            cp_cc[kv] = pl.at("CC")
    endphase()

    free("vtmp", "r_t2", "r_t1", "v_own", "k_own", "wqkv", "xn_fm",
         "p_qk1", "p_qk0", "p_tr1", "p_tr0")
    k_sb = alloc("k_sb", [P, 2 * 4096], BF16)
    v_sb = alloc("v_sb", [P, 2 * 4096], BF16)
    pt_sb = alloc("pt_sb", [P, 2 * 16384], BF16)
    a_sb = alloc("a_sb", [P, 2 * 4096], BF16)
    zbuf = alloc("zbuf", [P, 2 * 512], F32)
    rzb_sb = alloc("rzb_sb", [P, 2 * 512], BF16)
    ps_st = [palloc(f"ps_st{i}", [P, 512], F32) for i in range(4)]
    ps_pva = palloc("ps_pva", [P, 512], F32)
    ps_pvb = palloc("ps_pvb", [P, 512], F32)
    ps_pv = (ps_pva, ps_pvb)
    ps_z0 = palloc("ps_z0", [P, 512], F32)
    ps_z1 = palloc("ps_z1", [P, 512], F32)
    ps_zs = (ps_z0, ps_z1)

    cp_asm = {}
    cp_attn_pe = {}
    cp_aspill = {}
    st_free = [snap["SC"]] * 4
    ptsb_free = [snap["PE"], snap["PE"]]
    pv_free = [snap["VE"], snap["VE"]]
    z_free = [snap["VE"], snap["VE"]]
    zb_free = [snap["VE"], snap["VE"]]
    rzsb_free = [snap["VE"], snap["VE"]]
    asb_free = [("ASP0", 0), ("ASP1", 0)]

    def emit_assembly(kv):
        par = kv % 2
        ksl = k_sb[:, par * 4096:(par + 1) * 4096]
        vsl = v_sb[:, par * 4096:(par + 1) * 4096]
        kvo = d_kvout[kv]
        ksrc = kvo.rearrange("(j v2 d e) c -> v2 d j e c", j=4, v2=2, d=P, e=8)
        vsrc = kvo.rearrange("(j v2 w t) c -> v2 t j w c", j=4, v2=2, w=8, t=P)
        kdst = ksl.rearrange("d (w j t) -> d j w t", w=8, j=4, t=P)
        vdst = vsl.rearrange("p (w j c) -> p j w c", w=8, j=4, c=P)
        w = [("CC", cp_cc[kv])] + gate()
        if kv >= 2:
            w.append(("PE", cp_attn_pe[kv - 2]))
        asem = f"ASM{kv % 2}"
        for j in range(4):
            pl.op("sp", lambda s, a=kdst[:, j:j + 1, :, :],
                  bsrc=ksrc[0:1, :, j:j + 1, :, :]: s.dma_start(
                      out=a, in_=bsrc.rearrange("v2 d j e c -> d (v2 j) e c")),
                  waits=w if j == 0 else (), inc=asem, amt=16)
            pl.op("sp", lambda s, a=vdst[:, j:j + 1, :, :],
                  bsrc=vsrc[1:2, :, j:j + 1, :, :]: s.dma_start(
                      out=a, in_=bsrc.rearrange("v2 p j w c -> p (v2 j) w c")),
                  inc=asem, amt=16)
        cp_asm[kv] = (asem, pl.at(asem))

    emit_assembly(0)
    if debug:
        pl.op("sp", lambda s: s.dma_start(out=d_dbg_q[:], in_=q_fm[:, 0:TPC]),
              waits=[("VE", cp_qkv_ev[3])], inc="LDBG", amt=16)
        pl.op("sp", lambda s: s.dma_start(out=d_dbg_k[:], in_=k_sb[:, 0:TPC]),
              waits=[cp_asm[0]], inc="LDBG", amt=16)
        pl.op("sp", lambda s: s.dma_start(out=d_dbg_v[:], in_=v_sb[:, 0:TPC]),
              waits=[cp_asm[0]], inc="LDBG", amt=16)

    def attn_A(u, kv, qc):
        kpar = kv % 2
        ksl = k_sb[:, kpar * 4096:(kpar + 1) * 4096]
        vsl = v_sb[:, kpar * 4096:(kpar + 1) * 4096]
        v = qc
        nkc = 4 * (v + 1)
        upar = u % 2
        ptsl = pt_sb[:, upar * 16384:(upar + 1) * 16384]
        qsl = q_fm[:, (kv * 8 + qc) * 512:(kv * 8 + qc + 1) * 512]
        cp_exp = {}
        for kc in range(nkc):
            slot = kc % 4
            stp = ps_st[slot]
            w = [("SC", st_free[slot])]
            if kc == 0:
                w += [cp_asm[kv], ("VE", cp_qkv_ev[4 * kv + 3]),
                      ("PE", ptsb_free[upar])]
                if u < 2:
                    w += gate()
            pl.op("pe", lambda t, stp=stp, ksl=ksl, kc=kc, qsl=qsl: t.matmul(
                stp[:], ksl[:, kc * P:(kc + 1) * P], qsl,
                start=True, stop=True), waits=w, inc="PE")
            cp_st = pl.at("PE")
            w_exp = [("PE", cp_st)]
            if kc >= nkc - 4:
                kc_rel = kc - (nkc - 4)
                pl.op("ve", lambda vv, stp=stp, kc_rel=kc_rel: vv.tensor_add(
                    stp[:], stp[:], c_mkT[:, kc_rel * 512:(kc_rel + 1) * 512]),
                      waits=[("PE", cp_st)], inc="VE")
                w_exp = [("VE", pl.at("VE"))]
            pl.op("sc", lambda s, stp=stp, ptsl=ptsl, kc=kc: s.activation(
                out=ptsl[:, kc * 512:(kc + 1) * 512], in_=stp[:], func=AF.Exp,
                bias=c_nb[:, 0:1]), waits=w_exp, inc="SC")
            st_free[slot] = pl.at("SC")
            cp_exp[kc] = pl.at("SC")
        pvpar = u % 2
        pvp = ps_pv[pvpar]
        zp = ps_zs[u % 2]
        for kc in range(nkc):
            w = [("SC", cp_exp[kc])]
            if kc == 0:
                w.append(("VE", pv_free[pvpar]))
            pl.op("pe", lambda t, pvp=pvp, vsl=vsl, ptsl=ptsl, kc=kc, nkc=nkc:
                  t.matmul(pvp[:], vsl[:, kc * P:(kc + 1) * P],
                           ptsl[:, kc * 512:(kc + 1) * 512],
                           start=(kc == 0), stop=(kc == nkc - 1)),
                  waits=w, inc="PE")
            w = []
            if kc == 0:
                w.append(("VE", z_free[u % 2]))
            pl.op("pe", lambda t, zp=zp, ptsl=ptsl, kc=kc, nkc=nkc: t.matmul(
                zp[0:1, :], c_ones[:, 0:1], ptsl[:, kc * 512:(kc + 1) * 512],
                start=(kc == 0), stop=(kc == nkc - 1)), waits=w, inc="PE")
        cp_pvz = pl.at("PE")
        ptsb_free[upar] = cp_pvz
        if qc == NCH - 1 and kv + 1 < NKVH:
            emit_assembly(kv + 1)
        return dict(u=u, kv=kv, qc=qc, upar=upar, pvp=pvp, zp=zp,
                    pvpar=pvpar, cp_pvz=cp_pvz)

    def attn_B(c):
        u, kv, qc = c["u"], c["kv"], c["qc"]
        upar, pvp, zp, pvpar, cp_pvz = (c["upar"], c["pvp"], c["zp"],
                                        c["pvpar"], c["cp_pvz"])
        hpar = kv % 2
        asl = a_sb[:, hpar * 4096:(hpar + 1) * 4096]
        zsl = zbuf[:, upar * 512:(upar + 1) * 512]
        rzsl = rzb_sb[:, upar * 512:(upar + 1) * 512]
        pl.op("ve", lambda vv, zsl=zsl, zp=zp: vv.tensor_copy(
            zsl[0:1, :], zp[0:1, :]),
              waits=[("PE", cp_pvz), ("VE", zb_free[upar])], inc="VE")
        cp_zev = pl.at("VE")
        pl.op("ve", lambda vv, zsl=zsl: vv.reciprocal(zsl[0:1, :], zsl[0:1, :]),
              inc="VE", drain=True)
        zb_free[upar] = pl.at("VE")
        pl.op("ve", lambda vv, zsl=zsl, rzsl=rzsl: vv.tensor_copy(
            rzsl[0:1, :], zsl[0:1, :]),
              waits=[("VE", rzsb_free[upar])], inc="VE", drain=True)
        cp_rz = pl.at("VE")
        pl.op("pe", lambda t, rzsl=rzsl, zp=zp: t.matmul(
            zp[:], c_ones[0:1, :], rzsl[0:1, :], start=True, stop=True),
              waits=[("VE", cp_rz)], inc="PE")
        cp_rzbmm = pl.at("PE")
        pl.op("ve", lambda vv, rzsl=rzsl, zp=zp: vv.tensor_copy(rzsl, zp[:]),
              waits=[("PE", cp_rzbmm)], inc="VE")
        z_free[u % 2] = pl.at("VE")
        pl.op("ve", lambda vv, asl=asl, qc=qc, pvp=pvp, rzsl=rzsl: vv.tensor_tensor(
            asl[:].rearrange("p (hp t) -> p hp t", hp=4)[:, :, qc * P:(qc + 1) * P],
            pvp[:].rearrange("p (hp q) -> p hp q", hp=4),
            rzsl.rearrange("p (hp q) -> p hp q", hp=4), AluOpType.mult),
              waits=[asb_free[hpar]], inc="VE")
        pv_free[pvpar] = pl.at("VE")
        rzsb_free[upar] = pl.at("VE")
        if qc == NCH - 1:
            cp_attn_pe[kv] = cp_pvz
            cp_a_ev = pl.at("VE")
            aspn = f"ASP{kv % 2}"
            pl.op("sp", lambda s, asl=asl, kv=kv: s.dma_start(
                out=d_attn[kv:kv + 1, :, :].rearrange("h p t -> p h t"),
                in_=asl.rearrange("p (o t) -> p o t", o=1)),
                  waits=[("VE", cp_a_ev)], inc=aspn, amt=16)
            cp_aspill[kv] = (aspn, pl.at(aspn))
            if debug and kv == 0:
                pl.op("sp", lambda s, asl=asl: s.dma_start(
                    out=d_dbg_a[:], in_=asl[:, 0:TPC]),
                      waits=[("VE", cp_a_ev)], inc="LDBG", amt=16)
            asb_free[hpar] = cp_aspill[kv]

    u = 0
    pend = []
    for kv in range(NKVH):
        for qc in range(NCH):
            pend.append(attn_A(u, kv, qc))
            u += 1
            if len(pend) > 1:
                attn_B(pend.pop(0))
    for c in pend:
        attn_B(c)
    endphase()

    free("rzb_sb", "zbuf", "a_sb", "pt_sb", "v_sb", "k_sb", "q_fm",
         "ps_z1", "ps_z0", "ps_pvb", "ps_pva",
         "ps_st3", "ps_st2", "ps_st1", "ps_st0")
    a_all = alloc("a_all", [P, NQH * TPC], BF16)
    wout = alloc("wout", [P, 2 * 32 * P], BF16)
    op_sb = alloc("op_sb", [P, 2 * TPC], F32)
    ps_opa = palloc("ps_opa", [P, 1024], F32)
    ps_opb = palloc("ps_opb", [P, 1024], F32)
    ps_op = (ps_opa, ps_opb)

    for kvq in range(NKVH):
        w = [cp_aspill[kvq]]
        if kvq == 0:
            w += gate()
        pl.op("sp", lambda s, kvq=kvq: s.dma_start(
            out=a_all[:, kvq * 4 * TPC:(kvq + 1) * 4 * TPC].rearrange(
                "p (o t) -> p o t", o=1),
            in_=d_attn[kvq:kvq + 1, :, :].rearrange("h p t -> p h t")),
            waits=w, inc="REL", amt=16)
    cp_rel_all = ("REL", pl.at("REL"))

    d_wout4 = d_wout.rearrange("fc dc d f -> fc d dc f")
    cp_op_mm = {}
    cp_op_ev = {}
    cp_p0 = {}
    for foc in range(32):
        par = foc % 2
        wsl = wout[:, par * 32 * P:(par + 1) * 32 * P]
        w = [("PE", cp_op_mm[foc - 2])] if foc >= 2 else gate()
        won = f"WO{foc % 2}"
        pl.op("gp", lambda g, a=wsl, foc=foc: g.dma_start(
            out=a.rearrange("p (n dc f) -> p n dc f", n=1, dc=32),
            in_=d_wout4[foc:foc + 1, :, :, :].rearrange("fc d dc f -> d fc dc f")),
              waits=w, inc=won, amt=16)
        wd = (won, pl.at(won))
        pp = ps_op[par]
        w = [wd, cp_rel_all] + (gate() if foc == 0 else [])
        if foc >= 2:
            w.append(("VE", cp_op_ev[foc - 2]))

        def op_fn(t, wsl=wsl, pp=pp):
            last = None
            for dc in range(32):
                for tg in range(2):
                    last = t.matmul(
                        pp[:, tg * 512:(tg + 1) * 512],
                        wsl[:, dc * P:(dc + 1) * P],
                        a_all[:, dc * TPC + tg * 512: dc * TPC + (tg + 1) * 512],
                        start=(dc == 0), stop=(dc == 31))
            return last
        pl.op("pe", op_fn, waits=w, inc="PE")
        cp_op_mm[foc] = pl.at("PE")
        osl = op_sb[:, par * TPC:(par + 1) * TPC]
        w = [("PE", cp_op_mm[foc])]
        if foc >= 2:
            w.append(cp_p0[foc - 2])
        pl.op("ve", lambda vv, osl=osl, pp=pp: vv.tensor_copy(osl, pp[:]),
              waits=w, inc="VE")
        cp_op_ev[foc] = pl.at("VE")
        p0n = f"P0{foc % 2}"
        pl.op("sp", lambda s, osl=osl, foc=foc: s.dma_start(
            out=d_p0[foc:foc + 1, :, :].rearrange("f p t -> p f t"),
            in_=osl.rearrange("p (o t) -> p o t", o=1)),
              waits=[("VE", cp_op_ev[foc])], inc=p0n, amt=16)
        cp_p0[foc] = (p0n, pl.at(p0n))
        if debug and foc == 0:
            pl.op("sp", lambda s, osl=osl: s.dma_start(out=d_dbg_p0[:], in_=osl),
                  waits=[("VE", cp_op_ev[0])], inc="LDBG", amt=16)
    endphase()

    free("op_sb", "wout", "a_all", "ps_opb", "ps_opa")
    xn_fm2 = alloc("xn_fm2", [P, ECH * TPC], BF16)
    h_sb = alloc("h_sb", [P, THMAX * TPC], BF16)
    wg_sb = alloc("wg_sb", [P, 2 * ECH * P], BF16)
    wu_sb = alloc("wu_sb", [P, 2 * ECH * P], BF16)
    gs_sb = alloc("gs_sb", [P, 2 * TPC], BF16)

    pl.op("sp", lambda s: s.dma_start(out=xn_fm2[:], in_=d_xn[:]),
          waits=gate() + [cp_xn_spill], inc="LXR", amt=16)
    cp_xn_rel = ("LXR", pl.at("LXR"))
    xnv2 = xn_fm2[:].rearrange("p (ec t) -> p ec t", ec=ECH)

    d_wg4 = d_wg.rearrange("n ec e f -> n e ec f")
    d_wu4 = d_wu.rearrange("n ec e f -> n e ec f")
    d_wd4 = d_wd.rearrange("n m dff fo -> n dff m fo")

    wd_sb = pstr = dn_sb = None
    ps_dn = None
    cp_pn_prev = None
    dn_ev_final_prev = None
    pn_final_prev = None
    pchain = [d_p0, d_p1, d_p2, d_p3]
    for half in range(3):
        NH = THIRDS[half]
        base = sum(THIRDS[:half])
        p_prev, p_next = pchain[half], pchain[half + 1]
        ps_g0 = palloc(f"ps_g0_{half}", [P, 1024], F32)
        ps_g1 = palloc(f"ps_g1_{half}", [P, 1024], F32)
        ps_u0 = palloc(f"ps_u0_{half}", [P, 1024], F32)
        ps_u1 = palloc(f"ps_u1_{half}", [P, 1024], F32)
        ps_g = (ps_g0, ps_g1)
        ps_u = (ps_u0, ps_u1)
        cp_gu_mm = {}
        cp_gu_ev = {}
        for dl in range(NH):
            dffc = base + dl
            par = dl % 2
            wgs = wg_sb[:, par * ECH * P:(par + 1) * ECH * P]
            wus = wu_sb[:, par * ECH * P:(par + 1) * ECH * P]
            if dl >= 2:
                w = [("PE", cp_gu_mm[dl - 2])]
            else:
                w = [("PE", snap["PE"])]
            pl.op("gp", lambda g, a=wgs, dffc=dffc: g.dma_start(
                out=a.rearrange("p (n ec f) -> p n ec f", n=1, ec=ECH),
                in_=d_wg4[dffc:dffc + 1, :, :, :].rearrange("n e ec f -> e n ec f")),
                  waits=w, inc=f"WG{par}", amt=16)
            pl.op("gp", lambda g, a=wus, dffc=dffc: g.dma_start(
                out=a.rearrange("p (n ec f) -> p n ec f", n=1, ec=ECH),
                in_=d_wu4[dffc:dffc + 1, :, :, :].rearrange("n e ec f -> e n ec f")),
                  inc=f"WU{par}", amt=16)
            wd2 = [(f"WG{par}", pl.at(f"WG{par}")), (f"WU{par}", pl.at(f"WU{par}"))]
            gp_ = ps_g[par]
            up_ = ps_u[par]
            w = list(wd2)
            if dl == 0:
                w.append(cp_xn_rel)
                w += gate()
            if dl >= 2:
                w.append(("VE", cp_gu_ev[dl - 2]))

            def gu_fn(t, wgs=wgs, wus=wus, gp_=gp_, up_=up_):
                last = None
                for ec in range(ECH):
                    for tg in range(2):
                        t.matmul(gp_[:, tg * 512:(tg + 1) * 512],
                                 wgs[:, ec * P:(ec + 1) * P],
                                 xnv2[:, ec, tg * 512:(tg + 1) * 512],
                                 start=(ec == 0), stop=(ec == ECH - 1))
                        last = t.matmul(up_[:, tg * 512:(tg + 1) * 512],
                                        wus[:, ec * P:(ec + 1) * P],
                                        xnv2[:, ec, tg * 512:(tg + 1) * 512],
                                        start=(ec == 0), stop=(ec == ECH - 1))
                return last
            pl.op("pe", gu_fn, waits=w, inc="PE")
            cp_gu_mm[dl] = pl.at("PE")
            gsl = gs_sb[:, par * TPC:(par + 1) * TPC]
            pl.op("sc", lambda s, gsl=gsl, gp_=gp_, dffc=dffc: s.activation(
                out=gsl, in_=gp_[:], func=AF.Silu,
                scale=c_gsc[:, dffc:dffc + 1]),
                  waits=[("PE", cp_gu_mm[dl])], inc="SC")
            sc_si = pl.at("SC")
            pl.op("ve", lambda vv, dl=dl, up_=up_, gsl=gsl, dffc=dffc:
                  vv.scalar_tensor_tensor(
                      h_sb[:, dl * TPC:(dl + 1) * TPC], up_[:],
                      c_usc[:, dffc:dffc + 1], gsl,
                      AluOpType.mult, AluOpType.mult),
                  waits=[("SC", sc_si)], inc="VE")
            cp_gu_ev[dl] = pl.at("VE")
            if debug and half == 0 and dl == 0:
                pl.op("sp", lambda s: s.dma_start(out=d_dbg_h[:], in_=h_sb[:, 0:TPC]),
                      waits=[("VE", cp_gu_ev[0])], inc="LDBG", amt=16)
        cp_gu_all = pl.at("VE")
        endphase()

        free(f"ps_u1_{half}", f"ps_u0_{half}", f"ps_g1_{half}", f"ps_g0_{half}")
        if half == 0:
            wd_sb = alloc("wd_sb", [P, 2 * THMAX * P], BF16)
            pstr = alloc("pstr", [P, 2 * TPC], F32)
            dn_sb = alloc("dn_sb", [P, 2 * TPC], F32)
        ps_dn0 = palloc(f"ps_dn0_{half}", [P, 1024], F32)
        ps_dn1 = palloc(f"ps_dn1_{half}", [P, 1024], F32)
        ps_dn = (ps_dn0, ps_dn1)
        cp_dn_mm = {}
        cp_dn_ev = {}
        cp_pn = {}
        cp_pstr = {}
        for foc in range(32):
            par = foc % 2
            wds = wd_sb[:, par * THMAX * P:(par + 1) * THMAX * P]
            if foc >= 2:
                w = [("PE", cp_dn_mm[foc - 2])]
            else:
                w = [("PE", snap["PE"])] + (gate() if half == 0 else [])
            pl.op("gp", lambda g, a=wds[:, 0:NH * P], foc=foc, NH=NH, base=base:
                  g.dma_start(
                      out=a.rearrange("p (n m f) -> p n m f", n=1, m=NH),
                      in_=d_wd4[foc:foc + 1, :, base:base + NH, :].rearrange(
                          "n dff m fo -> dff n m fo")),
                  waits=w, inc=f"WN{par}", amt=16)
            wd = (f"WN{par}", pl.at(f"WN{par}"))
            psl = pstr[:, par * TPC:(par + 1) * TPC]
            w = [cp_p0[foc] if half == 0 else cp_pn_prev[foc]]
            if foc >= 2:
                w.append(("VE", cp_dn_ev[foc - 2]))
            elif half == 0:
                w += gate()
            else:
                w.append(("VE", dn_ev_final_prev))
            pl.op("sp", lambda s, psl=psl, foc=foc, src=p_prev: s.dma_start(
                out=psl.rearrange("p (o t) -> p o t", o=1),
                in_=src[foc:foc + 1, :, :].rearrange("f p t -> p f t")),
                  waits=w, inc=f"PS{par}", amt=16)
            cp_pstr[foc] = (f"PS{par}", pl.at(f"PS{par}"))
            pp = ps_dn[par]
            w = [wd, ("VE", cp_gu_all)]
            if foc >= 2:
                w.append(("VE", cp_dn_ev[foc - 2]))

            def dn_fn(t, wds=wds, pp=pp, NH=NH):
                last = None
                for dl in range(NH):
                    for tg in range(2):
                        last = t.matmul(
                            pp[:, tg * 512:(tg + 1) * 512],
                            wds[:, dl * P:(dl + 1) * P],
                            h_sb[:, dl * TPC + tg * 512: dl * TPC + (tg + 1) * 512],
                            start=(dl == 0), stop=(dl == NH - 1))
                return last
            pl.op("pe", dn_fn, waits=w, inc="PE")
            cp_dn_mm[foc] = pl.at("PE")
            dsl = dn_sb[:, par * TPC:(par + 1) * TPC]
            w = [("PE", cp_dn_mm[foc]), cp_pstr[foc]]
            if foc >= 2:
                w.append(cp_pn[foc - 2])
            elif half >= 1:
                w += pn_final_prev
            pl.op("ve", lambda vv, dsl=dsl, pp=pp, psl=psl, foc=foc:
                  vv.scalar_tensor_tensor(dsl, pp[:], c_dsc[:, foc:foc + 1],
                                          psl, AluOpType.mult, AluOpType.add),
                  waits=w, inc="VE")
            cp_dn_ev[foc] = pl.at("VE")
            pnn = f"PN{half}{par}"
            pl.op("sp", lambda s, dsl=dsl, foc=foc, dst=p_next: s.dma_start(
                out=dst[foc:foc + 1, :, :].rearrange("f p t -> p f t"),
                in_=dsl.rearrange("p (o t) -> p o t", o=1)),
                  waits=[("VE", cp_dn_ev[foc])], inc=pnn, amt=16)
            cp_pn[foc] = (pnn, pl.at(pnn))
        cp_pn_prev = dict(cp_pn)
        dn_ev_final_prev = pl.at("VE")
        pn_final_prev = [(f"PN{half}0", pl.at(f"PN{half}0")),
                         (f"PN{half}1", pl.at(f"PN{half}1"))]
        free(f"ps_dn1_{half}", f"ps_dn0_{half}")
        endphase()

    free("dn_sb", "pstr", "wd_sb", "gs_sb", "wu_sb", "wg_sb", "h_sb", "xn_fm2")
    f_sb = alloc("f_sb", [P, 2 * E], F32)
    xt2 = alloc("xt2", [P, 2 * E], F32)
    o_sb = alloc("o_sb", [P, 2 * E], F32)
    ps_fta = palloc("ps_fta", [P, 512], F32)
    ps_ftb = palloc("ps_ftb", [P, 512], F32)
    ps_ft = (ps_fta, ps_ftb)

    cp_ft = {}
    cp_oev = {}
    cp_st = {}
    p2_all = [(f"PN20", pl.at("PN20")), (f"PN21", pl.at("PN21"))]
    ft_free = [pl.at("VE"), pl.at("VE")]
    fgi = 0
    for tc in range(NCH):
        par = tc % 2
        fsl = f_sb[:, par * E:(par + 1) * E]
        xsl = xt2[:, par * E:(par + 1) * E]
        osl = o_sb[:, par * E:(par + 1) * E]
        w = list(p2_all) + (gate() if tc == 0 else [])
        if tc >= 2:
            w.append(("PE", cp_ft[(tc - 2, 7)]))
        fv = fsl.rearrange("p (foc t) -> p foc t", foc=32)
        fln = f"FL{tc % 2}"
        for q in range(4):
            pl.op("sp", lambda s, fv=fv, q=q, tc=tc: s.dma_start(
                out=fv[:, 8 * q:8 * q + 8, :],
                in_=d_p3[8 * q:8 * q + 8, :, tc * P:(tc + 1) * P].rearrange(
                    "foc p t -> p foc t")),
                  waits=w if q == 0 else (), inc=fln, amt=16)
        cp_f = (fln, pl.at(fln))
        w = [("VE", cp_oev[(tc - 2, 7)])] if tc >= 2 else (gate() if tc == 0 else [])
        pl.op("sp", lambda s, xsl=xsl, tc=tc: s.dma_start(
            out=xsl, in_=d_x[tc * P:(tc + 1) * P, :]), waits=w,
              inc=f"X2{tc % 2}", amt=16)
        cp_ldx2 = (f"X2{tc % 2}", pl.at(f"X2{tc % 2}"))
        for g in range(8):
            gpar = fgi % 2
            fgi += 1
            fp = ps_ft[gpar]
            w = [cp_f, ("VE", ft_free[gpar])]

            def ft_fn(t, fsl=fsl, fp=fp, g=g):
                last = None
                for i2 in range(4):
                    foc = 4 * g + i2
                    last = t.transpose(fp[:, i2 * P:(i2 + 1) * P],
                                       fsl[:, foc * P:(foc + 1) * P], c_idf[:])
                return last
            pl.op("pe", ft_fn, waits=w, inc="PE")
            cp_ft[(tc, g)] = pl.at("PE")
            w = [("PE", cp_ft[(tc, g)]), cp_ldx2]
            if g == 0 and tc >= 2:
                w.append(cp_st[tc - 2])
            pl.op("ve", lambda vv, osl=osl, xsl=xsl, fp=fp, g=g: vv.tensor_add(
                osl[:, 512 * g:512 * (g + 1)], fp[:], xsl[:, 512 * g:512 * (g + 1)]),
                  waits=w, inc="VE")
            cp_oev[(tc, g)] = pl.at("VE")
            ft_free[gpar] = cp_oev[(tc, g)]
        stn = f"ST{tc % 2}"
        pl.op("sp", lambda s, osl=osl, tc=tc: s.dma_start(
            out=d_out[tc * P:(tc + 1) * P, :], in_=osl),
              waits=[("VE", cp_oev[(tc, 7)])], inc=stn, amt=16)
        cp_st[tc] = (stn, pl.at(stn))

    pl.op("sp", lambda s: s.nop(),
          waits=[("ST0", pl.at("ST0")), ("ST1", pl.at("ST1"))])

    sems = {}
    sem_cms = []
    for name in sems_all:
        cm = nc.semaphore(name)
        sems[name] = cm.__enter__()
        sem_cms.append(cm)

    blk_cm = nc.Block()
    block = blk_cm.__enter__()

    def runner(engname):
        def run(eng):
            for waits, fn, inc, amt, drain in pl.ops[engname]:
                for s, vv in waits:
                    eng.wait_ge(sems[s], vv)
                inst = fn(eng)
                if inc is not None:
                    inst.then_inc(sems[inc], amt)
                if drain:
                    eng.drain()
        return run

    block.tensor(runner("pe"))
    block.vector(runner("ve"))
    block.scalar(runner("sc"))
    block.sync(runner("sp"))
    block.gpsimd(runner("gp"))

    blk_cm.__exit__(None, None, None)
    for cm in sem_cms:
        cm.__exit__(None, None, None)
    for cm in reversed(list(open_cms.values())):
        cm.__exit__(None, None, None)
    return nc


def _host_prep(L0_attn_x, L0_attn_w_in, L0_attn_w_out, L0_ffn_gate_w,
               L0_ffn_gate_s, L0_ffn_up_w, L0_ffn_up_s, L0_ffn_down_w,
               L0_ffn_down_s):
    bf = ml_dtypes.bfloat16
    x = np.ascontiguousarray(np.asarray(L0_attn_x, dtype=np.float32))
    w_in = np.asarray(L0_attn_w_in, dtype=np.float32)
    w_out = np.asarray(L0_attn_w_out, dtype=np.float32)
    gw = np.asarray(L0_ffn_gate_w, dtype=np.int32)
    uw = np.asarray(L0_ffn_up_w, dtype=np.int32)
    dw = np.asarray(L0_ffn_down_w, dtype=np.int32)

    w_in_t = np.ascontiguousarray(w_in.reshape(FCQKV, P, ECH, P).transpose(0, 2, 3, 1))
    w_out_t = np.ascontiguousarray(w_out.reshape(32, P, 32, P).transpose(0, 2, 3, 1))
    gate_t = np.ascontiguousarray(gw.reshape(DFFC, P, ECH, P).transpose(0, 2, 3, 1))
    up_t = np.ascontiguousarray(uw.reshape(DFFC, P, ECH, P).transpose(0, 2, 3, 1))
    down_t = np.ascontiguousarray(dw.reshape(32, P, DFFC, P).transpose(0, 2, 3, 1))
    gsc = np.ascontiguousarray(np.asarray(L0_ffn_gate_s, np.float32).reshape(DFFC, P).T)
    usc = np.ascontiguousarray(np.asarray(L0_ffn_up_s, np.float32).reshape(DFFC, P).T)
    dsc = np.ascontiguousarray(np.asarray(L0_ffn_down_s, np.float32).reshape(32, P).T)
    identb = np.eye(P, dtype=np.float32).astype(bf)
    identf = np.eye(P, dtype=np.float32)

    inv_freq = (1.0 / 10000.0 ** (np.arange(0, HD, 2, dtype=np.float32) / HD))

    in_maps = []
    row_idx = []
    for c in range(8):
        b, j = c // 4, c % 4
        rows = np.concatenate([np.arange(P) + (4 * v + j) * P for v in range(NCH)])
        row_idx.append((b, rows))
        xc = np.ascontiguousarray(x[b][rows])
        tglob = rows.astype(np.float32)
        ang = tglob[None, :] * inv_freq[:, None]
        rs = float(SCALE ** 0.5)
        cos = np.cos(ang) * rs
        sin = np.sin(ang) * rs
        costab = np.ascontiguousarray(np.concatenate([cos, cos], axis=0).astype(bf))
        sintab = np.ascontiguousarray(np.concatenate([-sin, sin], axis=0).astype(bf))
        maskt = np.zeros((P, 512), dtype=np.float32)
        ppg, kkg = np.meshgrid(np.arange(P), np.arange(512), indexing="ij")
        maskt[kkg > (128 * j + ppg)] = NEG
        masktt = np.zeros((P, 2048), dtype=np.float32)
        kk1, qq1 = np.meshgrid(np.arange(P), np.arange(P), indexing="ij")
        for kc_rel in range(4):
            if kc_rel < j:
                m = np.zeros((P, P), dtype=np.float32)
            elif kc_rel == j:
                m = np.where(kk1 <= qq1, 0.0, NEG).astype(np.float32)
            else:
                m = np.full((P, P), NEG, dtype=np.float32)
            for hp in range(4):
                masktt[:, kc_rel * 512 + hp * P:kc_rel * 512 + (hp + 1) * P] = m
        in_maps.append(dict(
            x=xc, w_in_t=w_in_t, w_out_t=w_out_t, gate_t=gate_t, up_t=up_t,
            down_t=down_t, gsc=gsc, usc=usc, dsc=dsc, costab=costab,
            sintab=sintab, maskt=maskt, identb=identb, identf=identf,
            negb=np.full((P, 1), -40.0, dtype=np.float32),
            onest=np.ones((P, P), dtype=np.float32).astype(bf),
            masktt=masktt))
    return in_maps, row_idx


_NC_CACHE = {}


def _run(inputs, debug=False):
    in_maps, row_idx = _host_prep(**inputs)
    key = ("nc", debug)
    if key not in _NC_CACHE:
        _NC_CACHE[key] = build_nc(debug=debug)
    nc = _NC_CACHE[key]
    res = run_bass_kernel_spmd(nc, in_maps, core_ids=list(range(8)))
    out = np.empty((2, 4096, 4096), dtype=np.float32)
    for c in range(8):
        b, rows = row_idx[c]
        out[b][rows] = np.asarray(res.results[c]["out"])
    return out, res


def kernel(**inputs):
    return _run(inputs, debug=False)[0]


if __name__ == "__main__":
    import reference as ref
    ins = ref.setup_inputs(0)
    ins = {k: np.asarray(v) for k, v in ins.items()}
    got = kernel(**ins)
    want = np.asarray(ref.reference(**ins))
    num = float(np.mean((got - want) ** 2))
    den = float(np.mean(want ** 2))
    print("resid var ratio:", num / den)


# version 15
# speedup vs baseline: 1.0104x; 1.0104x over previous
import numpy as np
import ml_dtypes
import concourse.bass as bass
import concourse.mybir as mybir
from concourse.bass_utils import run_bass_kernel_spmd
from concourse.alu_op_type import AluOpType
import concourse.bass_utils as _bu

if not getattr(_bu, "_ldw_patch", False):
    _bu._ldw_patch = True
    _orig_run_command = _bu.run_command

    def _run_command_ldw(argv, **kw):
        argv = ["--enable-ldw-opt=true" if a == "--enable-ldw-opt=false" else a
                for a in argv]
        return _orig_run_command(argv, **kw)

    _bu.run_command = _run_command_ldw

F32 = mybir.dt.float32
BF16 = mybir.dt.bfloat16
I32 = mybir.dt.int32
AF = mybir.ActivationFunctionType
AX = mybir.AxisListType

P = 128
E = 4096
ECH = 32
DFF = 11008
DFFC = 86
THIRDS = (29, 29, 28)
THMAX = 29
FCQKV = 48
TPC = 1024
NCH = 8
NQH = 32
NKVH = 8
HD = 128
SCALE = 1.0 / float(np.sqrt(128.0))
EPS = 1e-5
NEG = -1e30

FCS = []
for _kv in range(NKVH):
    FCS.append(32 + _kv)
    FCS.append(40 + _kv)
FCS += list(range(32))

ENGINES = ("pe", "ve", "sc", "sp", "gp")


class Plan:
    def __init__(self):
        self.ops = {e: [] for e in ENGINES}
        self.cnt = {}

    def sem(self, *names):
        for n in names:
            self.cnt[n] = 0

    def op(self, eng, fn, waits=(), inc=None, amt=1, drain=False):
        self.ops[eng].append((tuple(waits), fn, inc, amt, drain))
        if inc is not None:
            self.cnt[inc] += amt
            return self.cnt[inc]
        return None

    def at(self, name):
        return self.cnt[name]


def build_nc(debug=False):
    nc = bass.Bass("TRN2", target_bir_lowering=False)

    d_x = nc.dram_tensor("x", [TPC, E], F32, kind="ExternalInput").ap()
    d_win = nc.dram_tensor("w_in_t", [FCQKV, ECH, P, P], F32, kind="ExternalInput").ap()
    d_wout = nc.dram_tensor("w_out_t", [32, 32, P, P], F32, kind="ExternalInput").ap()
    d_wg = nc.dram_tensor("gate_t", [DFFC, ECH, P, P], I32, kind="ExternalInput").ap()
    d_wu = nc.dram_tensor("up_t", [DFFC, ECH, P, P], I32, kind="ExternalInput").ap()
    d_wd = nc.dram_tensor("down_t", [32, DFFC, P, P], I32, kind="ExternalInput").ap()
    d_gsc = nc.dram_tensor("gsc", [P, DFFC], F32, kind="ExternalInput").ap()
    d_usc = nc.dram_tensor("usc", [P, DFFC], F32, kind="ExternalInput").ap()
    d_dsc = nc.dram_tensor("dsc", [P, 32], F32, kind="ExternalInput").ap()
    d_cos = nc.dram_tensor("costab", [P, TPC], BF16, kind="ExternalInput").ap()
    d_sin = nc.dram_tensor("sintab", [P, TPC], BF16, kind="ExternalInput").ap()
    d_mask = nc.dram_tensor("maskt", [P, 512], F32, kind="ExternalInput").ap()
    d_idb = nc.dram_tensor("identb", [P, P], BF16, kind="ExternalInput").ap()
    d_negb = nc.dram_tensor("negb", [P, 1], F32, kind="ExternalInput").ap()
    d_ones = nc.dram_tensor("onest", [P, P], BF16, kind="ExternalInput").ap()
    d_mkt = nc.dram_tensor("masktt", [P, 2048], F32, kind="ExternalInput").ap()
    d_idf = nc.dram_tensor("identf", [P, P], F32, kind="ExternalInput").ap()
    d_out = nc.dram_tensor("out", [TPC, E], F32, kind="ExternalOutput").ap()

    if debug:
        d_dbg_lt = nc.dram_tensor("dbg_lt", [P, 16], F32, kind="ExternalOutput").ap()
        d_dbg_xn = nc.dram_tensor("dbg_xn", [P, 2048], BF16, kind="ExternalOutput").ap()
        d_dbg_q = nc.dram_tensor("dbg_q", [P, TPC], BF16, kind="ExternalOutput").ap()
        d_dbg_k = nc.dram_tensor("dbg_k", [P, TPC], BF16, kind="ExternalOutput").ap()
        d_dbg_v = nc.dram_tensor("dbg_v", [P, TPC], BF16, kind="ExternalOutput").ap()
        d_dbg_p = nc.dram_tensor("dbg_p", [P, 512], BF16, kind="ExternalOutput").ap()
        d_dbg_a = nc.dram_tensor("dbg_a", [P, TPC], BF16, kind="ExternalOutput").ap()
        d_dbg_p0 = nc.dram_tensor("dbg_p0", [P, TPC], F32, kind="ExternalOutput").ap()
        d_dbg_h = nc.dram_tensor("dbg_h", [P, TPC], BF16, kind="ExternalOutput").ap()
    d_xn = nc.dram_tensor("xn_spill", [P, ECH * TPC], BF16, kind="Internal").ap()
    d_attn = nc.dram_tensor("attn_spill", [NKVH, P, 4 * TPC], BF16, kind="Internal").ap()
    d_p0 = nc.dram_tensor("p0", [32, P, TPC], F32, kind="Internal").ap()
    d_p1 = nc.dram_tensor("p1", [32, P, TPC], F32, kind="Internal").ap()
    d_p2 = nc.dram_tensor("p2", [32, P, TPC], F32, kind="Internal").ap()
    d_p3 = nc.dram_tensor("p3", [32, P, TPC], F32, kind="Internal").ap()
    d_kvin = [nc.dram_tensor(f"kvin{k}", [2048, P], BF16, kind="Internal").ap()
              for k in range(NKVH)]
    d_kvout = [nc.dram_tensor(f"kvout{k}", [8192, P], BF16, kind="Internal").ap()
               for k in range(NKVH)]

    pl = Plan()
    sems_all = (["LDC", "LDBG", "LSP", "LXR", "REL", "CC", "PE", "VE", "SC"]
                + [f"LX{i}" for i in range(2)]
                + [f"LST{i}" for i in range(8)]
                + [f"CC{i}" for i in range(8)]
                + [f"ASM{i}" for i in range(2)]
                + [f"ASP{i}" for i in range(2)]
                + [f"WQ{i}" for i in range(2)]
                + [f"WO{i}" for i in range(2)]
                + [f"WG{i}" for i in range(2)]
                + [f"WU{i}" for i in range(2)]
                + [f"WN{i}" for i in range(2)]
                + [f"PS{i}" for i in range(2)]
                + [f"P0{i}" for i in range(2)]
                + [f"PN0{i}" for i in range(2)]
                + [f"PN1{i}" for i in range(2)]
                + [f"PN2{i}" for i in range(2)]
                + [f"FL{i}" for i in range(2)]
                + [f"X2{i}" for i in range(2)]
                + [f"ST{i}" for i in range(2)])
    pl.sem(*sems_all)

    open_cms = {}

    def alloc(name, shape, dt, side="right"):
        cm = nc.sbuf_tensor(name, shape, dt, side=side)
        h = cm.__enter__()
        open_cms[name] = cm
        return h

    def palloc(name, shape, dt):
        cm = nc.psum_tensor(name, shape, dt)
        h = cm.__enter__()
        open_cms[name] = cm
        return h

    def free(*names):
        for n in names:
            open_cms.pop(n).__exit__(None, None, None)

    snap = {"PE": 0, "VE": 0, "SC": 0}

    def gate():
        return [("PE", snap["PE"]), ("VE", snap["VE"]), ("SC", snap["SC"])]

    def endphase():
        snap["PE"] = pl.at("PE")
        snap["VE"] = pl.at("VE")
        snap["SC"] = pl.at("SC")

    c_cos = alloc("c_cos", [P, TPC], BF16, side="left")
    c_sin = alloc("c_sin", [P, TPC], BF16, side="left")
    c_mask = alloc("c_mask", [P, 512], F32, side="left")
    c_idb = alloc("c_idb", [P, P], BF16, side="left")
    c_nb = alloc("c_nb", [P, 1], F32, side="left")
    c_ones = alloc("c_ones", [P, P], BF16, side="left")
    c_mkT = alloc("c_mkT", [P, 2048], F32, side="left")
    c_idf = alloc("c_idf", [P, P], F32, side="left")
    c_gsc = alloc("c_gsc", [P, DFFC], F32, side="left")
    c_usc = alloc("c_usc", [P, DFFC], F32, side="left")
    c_dsc = alloc("c_dsc", [P, 32], F32, side="left")
    for dst, src in [(c_cos, d_cos), (c_sin, d_sin), (c_mask, d_mask),
                     (c_idb, d_idb), (c_nb, d_negb), (c_ones, d_ones),
                     (c_mkT, d_mkt), (c_idf, d_idf), (c_gsc, d_gsc),
                     (c_usc, d_usc), (c_dsc, d_dsc)]:
        pl.op("sp", lambda s, a=dst, b=src: s.dma_start(out=a[:], in_=b[:]),
              inc="LDC", amt=16)

    xn_fm = alloc("xn_fm", [P, ECH * TPC], BF16)
    xt = alloc("xt", [P, 2 * E], F32)
    xn_tm = alloc("xn_tm", [P, 2 * E], BF16)
    lt = alloc("lt", [P, 16], F32)
    p_tr0 = palloc("p_tr0", [P, 1024], BF16)
    p_tr1 = palloc("p_tr1", [P, 1024], BF16)
    p_trs = (p_tr0, p_tr1)
    xnfm3 = xn_fm[:].rearrange("p (ec t) -> p ec t", ec=ECH)

    cp_lnxn = {}
    cp_ev0 = {}
    cp_trpe = {}
    for tc in range(NCH):
        par = tc % 2
        xts = xt[:, par * E:(par + 1) * E]
        xnts = xn_tm[:, par * E:(par + 1) * E]
        b = 8 * par
        C = lambda i, bb=b: lt[:, bb + i:bb + i + 1]
        lxs = f"LX{tc % 2}"
        w = [("VE", cp_lnxn[tc - 2])] if tc >= 2 else []
        pl.op("sp", lambda s, a=xts, t=tc: s.dma_start(
            out=a, in_=d_x[t * P:(t + 1) * P, :]), waits=w, inc=lxs, amt=16)
        ldx = (lxs, pl.at(lxs))
        w = [ldx] + ([("VE", cp_lnxn[tc - 2])] if tc >= 2 else [("LDC", pl.at("LDC"))])
        pl.op("ve", lambda v, a=xts, c=C(0): v.reduce_sum(c, a, axis=AX.X),
              waits=w, inc="VE", drain=True)
        w = [ldx] + ([("PE", cp_trpe[(tc - 2, 3)])] if tc >= 2 else [("LDC", pl.at("LDC"))])
        pl.op("sc", lambda s, a=xts, o=xnts, c=C(1): s.activation(
            out=o, in_=a, func=AF.Square, accum_out=c), waits=w, inc="SC")
        sc_sq = pl.at("SC")
        pl.op("ve", lambda v, c0=C(0), c2=C(2): v.tensor_scalar_mul(c2, c0, 1.0 / E),
              inc="VE", drain=True)
        pl.op("ve", lambda v, c1=C(1), c3=C(3): v.tensor_scalar_mul(c3, c1, 1.0 / E),
              waits=[("SC", sc_sq)], inc="VE", drain=True)
        ve_m = pl.at("VE")
        pl.op("sc", lambda s, c2=C(2), c4=C(4): s.activation(out=c4, in_=c2, func=AF.Square),
              waits=[("VE", ve_m)], inc="SC")
        sc_m2 = pl.at("SC")
        pl.op("ve", lambda v, c3=C(3), c4=C(4), c5=C(5): v.tensor_sub(c5, c3, c4),
              waits=[("SC", sc_m2)], inc="VE", drain=True)
        ve_var = pl.at("VE")
        pl.op("ve", lambda v, c5=C(5): v.tensor_scalar_add(c5, c5, EPS), inc="VE",
              drain=True)
        ve_var2 = pl.at("VE")
        pl.op("sc", lambda s, c5=C(5), c6=C(6): s.activation(
            out=c6, in_=c5, func=AF.Ln), waits=[("VE", ve_var2)], inc="SC",
              drain=True)
        pl.op("sc", lambda s, c6=C(6): s.activation(out=c6, in_=c6, func=AF.Exp, scale=-0.5),
              inc="SC")
        sc_rstd = pl.at("SC")
        pl.op("ve", lambda v, c2=C(2), c6=C(6), c7=C(7): v.scalar_tensor_tensor(
            c7, c2, -1.0, c6, AluOpType.mult, AluOpType.mult),
              waits=[("SC", sc_rstd)], inc="VE", drain=True)
        ve_nmr = pl.at("VE")
        pl.op("ve", lambda v, a=xts, o=xnts, c6=C(6), c7=C(7): v.tensor_scalar(
            o, a, c6, c7, AluOpType.mult, AluOpType.add),
              waits=[("VE", ve_nmr)], inc="VE")
        cp_lnxn[tc] = pl.at("VE")

        for g in range(4):
            gpar = g % 2
            pp = p_trs[gpar]
            w = [("VE", cp_lnxn[tc])]
            if tc == 0 and g == 0:
                w.append(("LDC", pl.at("LDC")))
            prev = (tc, g - 2) if g >= 2 else ((tc - 1, g + 2) if tc >= 1 else None)
            if prev is not None:
                w.append(("VE", cp_ev0[prev]))

            def tr_fn(t, xn=xnts, pp=pp, g=g):
                last = None
                for i in range(8):
                    ec = 8 * g + i
                    last = t.transpose(pp[:, i * P:(i + 1) * P],
                                       xn[:, ec * P:(ec + 1) * P], c_idb[:])
                return last
            pl.op("pe", tr_fn, waits=w, inc="PE")
            cp_trpe[(tc, g)] = pl.at("PE")
            pl.op("ve", lambda v, pp=pp, g=g, tc=tc: v.tensor_copy(
                xnfm3[:, 8 * g:8 * g + 8, tc * P:(tc + 1) * P], pp[:]),
                  waits=[("PE", cp_trpe[(tc, g)])], inc="VE")
            cp_ev0[(tc, g)] = pl.at("VE")
    cp_xnfm_done = pl.at("VE")
    if debug:
        pl.op("sp", lambda s: s.dma_start(out=d_dbg_lt[:], in_=lt[:]),
              waits=[("VE", cp_lnxn[0])], inc="LDBG", amt=16)
        pl.op("sp", lambda s: s.dma_start(out=d_dbg_xn[:], in_=xn_fm[:, 0:2048]),
              waits=[("VE", cp_xnfm_done)], inc="LDBG", amt=16)
    endphase()

    free("lt", "xn_tm", "xt")
    wqkv = alloc("wqkv", [P, 2 * ECH * P], BF16)
    q_fm = alloc("q_fm", [P, NQH * TPC], BF16, side="left")
    k_own = alloc("k_own", [P, NKVH * TPC], BF16)
    v_own = alloc("v_own", [P, NKVH * TPC], BF16)
    r_t1 = alloc("r_t1", [P, TPC], BF16)
    r_t2 = alloc("r_t2", [P, TPC], BF16)
    vtmp = alloc("vtmp", [P, TPC], BF16)
    p_qk0 = palloc("p_qk0", [P, 1024], F32)
    p_qk1 = palloc("p_qk1", [P, 1024], F32)
    p_qks = (p_qk0, p_qk1)

    pl.op("sp", lambda s: s.dma_start(out=d_xn[:], in_=xn_fm[:]),
          waits=[("VE", cp_xnfm_done)], inc="LSP", amt=16)
    cp_xn_spill = ("LSP", pl.at("LSP"))

    d_win4 = d_win.rearrange("fc ec e f -> fc e ec f")
    xnv = xnfm3

    cp_qkv_mm = {}
    cp_qkv_ev = {}
    cp_stage = {}
    cp_cc = {}
    v_evac = {}
    vtr_prev = None
    for i, fc in enumerate(FCS):
        par = i % 2
        wsl = wqkv[:, par * ECH * P:(par + 1) * ECH * P]
        w = [("PE", cp_qkv_mm[FCS[i - 2]])] if i >= 2 else gate()
        wqs = f"WQ{i % 2}"
        pl.op("gp", lambda g, a=wsl, fc=fc: g.dma_start(
            out=a.rearrange("p (n ec f) -> p n ec f", n=1, ec=ECH),
            in_=d_win4[fc:fc + 1, :, :, :].rearrange("fc e ec f -> e fc ec f")),
              waits=w, inc=wqs, amt=16)
        wd = (wqs, pl.at(wqs))
        pp = p_qks[par]
        w0 = [wd]
        if i == 0:
            w0.append(("VE", cp_ev0[(3, 3)]))
        if i >= 2:
            w0.append(("VE", cp_qkv_ev[FCS[i - 2]]))

        def qkv_half(t, wsl=wsl, pp=pp, tg=0):
            last = None
            for ec in range(ECH):
                last = t.matmul(pp[:, tg * 512:(tg + 1) * 512],
                                wsl[:, ec * P:(ec + 1) * P],
                                xnv[:, ec, tg * 512:(tg + 1) * 512],
                                start=(ec == 0), stop=(ec == ECH - 1))
            return last
        pl.op("pe", lambda t, f=qkv_half: f(t, tg=0), waits=w0, inc="PE")
        w1 = [("VE", cp_xnfm_done)] if i == 0 else []
        pl.op("pe", lambda t, f=qkv_half: f(t, tg=1), waits=w1, inc="PE")
        cp_qkv_mm[fc] = pl.at("PE")

        if fc < 40:
            if fc < 32:
                kvi_, hpi_ = fc // 4, fc % 4
                q5 = q_fm[:].rearrange("p (kv qc hp q) -> p kv qc hp q",
                                       kv=8, qc=8, hp=4, q=P)
                dest = q5[:, kvi_, :, hpi_, :]
            else:
                dest = k_own[:, (fc - 32) * TPC:(fc - 31) * TPC].rearrange(
                    "p (qc q) -> p qc q", qc=8)
            pl.op("ve", lambda v, pp=pp: v.tensor_mul(r_t1[:], pp[:], c_cos[:]),
                  waits=[("PE", cp_qkv_mm[fc])], inc="VE")
            pl.op("ve", lambda v, pp=pp: v.tensor_mul(
                r_t2[0:64, :], pp[64:128, :], c_sin[0:64, :]), inc="VE")
            pl.op("ve", lambda v, pp=pp: v.tensor_mul(
                r_t2[64:128, :], pp[0:64, :], c_sin[64:128, :]), inc="VE",
                  drain=True)
            pl.op("ve", lambda v, d=dest: v.tensor_add(
                d, r_t1[:].rearrange("p (a q) -> p a q", q=P),
                r_t2[:].rearrange("p (a q) -> p a q", q=P)), inc="VE")
            cp_qkv_ev[fc] = pl.at("VE")
        else:
            kv = fc - 40
            w = [("PE", cp_qkv_mm[fc])]
            if vtr_prev is not None:
                w.append(("PE", vtr_prev))
            pl.op("ve", lambda v, pp=pp: v.tensor_copy(vtmp[:], pp[:]),
                  waits=w, inc="VE")
            cp_qkv_ev[fc] = pl.at("VE")
            vp = p_trs[kv % 2]
            w = [("VE", cp_qkv_ev[fc])]
            if kv >= 2:
                w.append(("VE", v_evac[kv - 2]))
            elif kv == 0:
                w.append(("VE", cp_xnfm_done))

            def vtr_fn(t, vp=vp):
                last = None
                for tcc in range(8):
                    last = t.transpose(vp[:, tcc * P:(tcc + 1) * P],
                                       vtmp[:, tcc * P:(tcc + 1) * P], c_idb[:])
                return last
            pl.op("pe", vtr_fn, waits=w, inc="PE")
            vtr_prev = pl.at("PE")
            pl.op("ve", lambda v, vp=vp, kv=kv: v.tensor_copy(
                v_own[:, kv * TPC:(kv + 1) * TPC], vp[:]),
                  waits=[("PE", vtr_prev)], inc="VE")
            v_evac[kv] = pl.at("VE")

            kvi = d_kvin[kv]
            k_view = kvi.rearrange("(v2 d e) c -> d v2 (e c)", v2=2, d=P, e=8)
            v_view = kvi.rearrange("(v2 tc p) c -> p v2 tc c", v2=2, tc=8, p=P)
            pl.op("sp", lambda s, kv=kv, k_view=k_view: s.dma_start(
                out=k_view[:, 0:1, :],
                in_=k_own[:, kv * TPC:(kv + 1) * TPC].rearrange(
                    "p (n t) -> p n t", n=1)),
                  waits=[("VE", cp_qkv_ev[32 + kv])], inc=f"LST{kv}", amt=16)
            pl.op("sp", lambda s, kv=kv, v_view=v_view: s.dma_start(
                out=v_view[:, 1:2, :, :],
                in_=v_own[:, kv * TPC:(kv + 1) * TPC].rearrange(
                    "p (n tc c) -> p n tc c", n=1, tc=8)),
                  waits=[("VE", v_evac[kv])], inc=f"LST{kv}", amt=16)
            cp_stage[kv] = (f"LST{kv}", pl.at(f"LST{kv}"))
            pl.op("gp", lambda g, kv=kv: g.collective_compute(
                "AllGather", AluOpType.bypass,
                replica_groups=[[0, 1, 2, 3], [4, 5, 6, 7]],
                ins=[d_kvin[kv][:]], outs=[d_kvout[kv][:]]),
                  waits=[cp_stage[kv]], inc=f"CC{kv}", amt=1)
            cp_cc[kv] = (f"CC{kv}", 1)
    endphase()

    free("vtmp", "r_t2", "r_t1", "v_own", "k_own", "wqkv", "xn_fm",
         "p_qk1", "p_qk0", "p_tr1", "p_tr0")
    k_sb = alloc("k_sb", [P, 2 * 4096], BF16)
    v_sb = alloc("v_sb", [P, 2 * 4096], BF16)
    pt_sb = alloc("pt_sb", [P, 2 * 16384], BF16)
    a_sb = alloc("a_sb", [P, 2 * 4096], BF16)
    zbuf = alloc("zbuf", [P, 2 * 512], F32)
    rzb_sb = alloc("rzb_sb", [P, 2 * 512], BF16)
    ps_st = [palloc(f"ps_st{i}", [P, 512], F32) for i in range(4)]
    ps_pva = palloc("ps_pva", [P, 512], F32)
    ps_pvb = palloc("ps_pvb", [P, 512], F32)
    ps_pv = (ps_pva, ps_pvb)
    ps_z0 = palloc("ps_z0", [P, 512], F32)
    ps_z1 = palloc("ps_z1", [P, 512], F32)
    ps_zs = (ps_z0, ps_z1)

    cp_asm = {}
    cp_attn_pe = {}
    cp_aspill = {}
    st_free = [snap["SC"]] * 4
    ptsb_free = [snap["PE"], snap["PE"]]
    pv_free = [snap["VE"], snap["VE"]]
    z_free = [snap["VE"], snap["VE"]]
    zb_free = [snap["VE"], snap["VE"]]
    rzsb_free = [snap["VE"], snap["VE"]]
    asb_free = [("ASP0", 0), ("ASP1", 0)]

    def emit_assembly(kv):
        par = kv % 2
        ksl = k_sb[:, par * 4096:(par + 1) * 4096]
        vsl = v_sb[:, par * 4096:(par + 1) * 4096]
        kvo = d_kvout[kv]
        ksrc = kvo.rearrange("(j v2 d e) c -> v2 d j e c", j=4, v2=2, d=P, e=8)
        vsrc = kvo.rearrange("(j v2 w t) c -> v2 t j w c", j=4, v2=2, w=8, t=P)
        kdst = ksl.rearrange("d (w j t) -> d j w t", w=8, j=4, t=P)
        vdst = vsl.rearrange("p (w j c) -> p j w c", w=8, j=4, c=P)
        w = [cp_cc[kv]] + gate()
        if kv >= 2:
            w.append(("PE", cp_attn_pe[kv - 2]))
        asem = f"ASM{kv % 2}"
        for j in range(4):
            pl.op("sp", lambda s, a=kdst[:, j:j + 1, :, :],
                  bsrc=ksrc[0:1, :, j:j + 1, :, :]: s.dma_start(
                      out=a, in_=bsrc.rearrange("v2 d j e c -> d (v2 j) e c")),
                  waits=w if j == 0 else (), inc=asem, amt=16)
            pl.op("sp", lambda s, a=vdst[:, j:j + 1, :, :],
                  bsrc=vsrc[1:2, :, j:j + 1, :, :]: s.dma_start(
                      out=a, in_=bsrc.rearrange("v2 p j w c -> p (v2 j) w c")),
                  inc=asem, amt=16)
        cp_asm[kv] = (asem, pl.at(asem))

    emit_assembly(0)
    if debug:
        pl.op("sp", lambda s: s.dma_start(out=d_dbg_q[:], in_=q_fm[:, 0:TPC]),
              waits=[("VE", cp_qkv_ev[3])], inc="LDBG", amt=16)
        pl.op("sp", lambda s: s.dma_start(out=d_dbg_k[:], in_=k_sb[:, 0:TPC]),
              waits=[cp_asm[0]], inc="LDBG", amt=16)
        pl.op("sp", lambda s: s.dma_start(out=d_dbg_v[:], in_=v_sb[:, 0:TPC]),
              waits=[cp_asm[0]], inc="LDBG", amt=16)

    def attn_A(u, kv, qc):
        kpar = kv % 2
        ksl = k_sb[:, kpar * 4096:(kpar + 1) * 4096]
        vsl = v_sb[:, kpar * 4096:(kpar + 1) * 4096]
        v = qc
        nkc = 4 * (v + 1)
        upar = u % 2
        ptsl = pt_sb[:, upar * 16384:(upar + 1) * 16384]
        qsl = q_fm[:, (kv * 8 + qc) * 512:(kv * 8 + qc + 1) * 512]
        cp_exp = {}
        for kc in range(nkc):
            slot = kc % 4
            stp = ps_st[slot]
            w = [("SC", st_free[slot])]
            if kc == 0:
                w += [cp_asm[kv], ("VE", cp_qkv_ev[4 * kv + 3]),
                      ("PE", ptsb_free[upar])]
                if u < 2:
                    w += gate()
            pl.op("pe", lambda t, stp=stp, ksl=ksl, kc=kc, qsl=qsl: t.matmul(
                stp[:], ksl[:, kc * P:(kc + 1) * P], qsl,
                start=True, stop=True), waits=w, inc="PE")
            cp_st = pl.at("PE")
            w_exp = [("PE", cp_st)]
            if kc >= nkc - 4:
                kc_rel = kc - (nkc - 4)
                pl.op("ve", lambda vv, stp=stp, kc_rel=kc_rel: vv.tensor_add(
                    stp[:], stp[:], c_mkT[:, kc_rel * 512:(kc_rel + 1) * 512]),
                      waits=[("PE", cp_st)], inc="VE")
                w_exp = [("VE", pl.at("VE"))]
            pl.op("sc", lambda s, stp=stp, ptsl=ptsl, kc=kc: s.activation(
                out=ptsl[:, kc * 512:(kc + 1) * 512], in_=stp[:], func=AF.Exp,
                bias=c_nb[:, 0:1]), waits=w_exp, inc="SC")
            st_free[slot] = pl.at("SC")
            cp_exp[kc] = pl.at("SC")
        pvpar = u % 2
        pvp = ps_pv[pvpar]
        zp = ps_zs[u % 2]
        for kc in range(nkc):
            w = [("SC", cp_exp[kc])]
            if kc == 0:
                w.append(("VE", pv_free[pvpar]))
            pl.op("pe", lambda t, pvp=pvp, vsl=vsl, ptsl=ptsl, kc=kc, nkc=nkc:
                  t.matmul(pvp[:], vsl[:, kc * P:(kc + 1) * P],
                           ptsl[:, kc * 512:(kc + 1) * 512],
                           start=(kc == 0), stop=(kc == nkc - 1)),
                  waits=w, inc="PE")
            w = []
            if kc == 0:
                w.append(("VE", z_free[u % 2]))
            pl.op("pe", lambda t, zp=zp, ptsl=ptsl, kc=kc, nkc=nkc: t.matmul(
                zp[0:1, :], c_ones[:, 0:1], ptsl[:, kc * 512:(kc + 1) * 512],
                start=(kc == 0), stop=(kc == nkc - 1)), waits=w, inc="PE")
        cp_pvz = pl.at("PE")
        ptsb_free[upar] = cp_pvz
        if qc == NCH - 1 and kv + 1 < NKVH:
            emit_assembly(kv + 1)
        return dict(u=u, kv=kv, qc=qc, upar=upar, pvp=pvp, zp=zp,
                    pvpar=pvpar, cp_pvz=cp_pvz)

    def attn_B(c):
        u, kv, qc = c["u"], c["kv"], c["qc"]
        upar, pvp, zp, pvpar, cp_pvz = (c["upar"], c["pvp"], c["zp"],
                                        c["pvpar"], c["cp_pvz"])
        hpar = kv % 2
        asl = a_sb[:, hpar * 4096:(hpar + 1) * 4096]
        zsl = zbuf[:, upar * 512:(upar + 1) * 512]
        rzsl = rzb_sb[:, upar * 512:(upar + 1) * 512]
        pl.op("ve", lambda vv, zsl=zsl, zp=zp: vv.tensor_copy(
            zsl[0:1, :], zp[0:1, :]),
              waits=[("PE", cp_pvz), ("VE", zb_free[upar])], inc="VE")
        cp_zev = pl.at("VE")
        pl.op("ve", lambda vv, zsl=zsl: vv.reciprocal(zsl[0:1, :], zsl[0:1, :]),
              inc="VE", drain=True)
        zb_free[upar] = pl.at("VE")
        pl.op("ve", lambda vv, zsl=zsl, rzsl=rzsl: vv.tensor_copy(
            rzsl[0:1, :], zsl[0:1, :]),
              waits=[("VE", rzsb_free[upar])], inc="VE", drain=True)
        cp_rz = pl.at("VE")
        pl.op("pe", lambda t, rzsl=rzsl, zp=zp: t.matmul(
            zp[:], c_ones[0:1, :], rzsl[0:1, :], start=True, stop=True),
              waits=[("VE", cp_rz)], inc="PE")
        cp_rzbmm = pl.at("PE")
        pl.op("ve", lambda vv, rzsl=rzsl, zp=zp: vv.tensor_copy(rzsl, zp[:]),
              waits=[("PE", cp_rzbmm)], inc="VE")
        z_free[u % 2] = pl.at("VE")
        pl.op("ve", lambda vv, asl=asl, qc=qc, pvp=pvp, rzsl=rzsl: vv.tensor_tensor(
            asl[:].rearrange("p (hp t) -> p hp t", hp=4)[:, :, qc * P:(qc + 1) * P],
            pvp[:].rearrange("p (hp q) -> p hp q", hp=4),
            rzsl.rearrange("p (hp q) -> p hp q", hp=4), AluOpType.mult),
              waits=[asb_free[hpar]], inc="VE")
        pv_free[pvpar] = pl.at("VE")
        rzsb_free[upar] = pl.at("VE")
        if qc == NCH - 1:
            cp_attn_pe[kv] = cp_pvz
            cp_a_ev = pl.at("VE")
            aspn = f"ASP{kv % 2}"
            pl.op("sp", lambda s, asl=asl, kv=kv: s.dma_start(
                out=d_attn[kv:kv + 1, :, :].rearrange("h p t -> p h t"),
                in_=asl.rearrange("p (o t) -> p o t", o=1)),
                  waits=[("VE", cp_a_ev)], inc=aspn, amt=16)
            cp_aspill[kv] = (aspn, pl.at(aspn))
            if debug and kv == 0:
                pl.op("sp", lambda s, asl=asl: s.dma_start(
                    out=d_dbg_a[:], in_=asl[:, 0:TPC]),
                      waits=[("VE", cp_a_ev)], inc="LDBG", amt=16)
            asb_free[hpar] = cp_aspill[kv]

    u = 0
    pend = []
    for kv in range(NKVH):
        for qc in range(NCH):
            pend.append(attn_A(u, kv, qc))
            u += 1
            if len(pend) > 1:
                attn_B(pend.pop(0))
    for c in pend:
        attn_B(c)
    endphase()

    free("rzb_sb", "zbuf", "a_sb", "pt_sb", "v_sb", "k_sb", "q_fm",
         "ps_z1", "ps_z0", "ps_pvb", "ps_pva",
         "ps_st3", "ps_st2", "ps_st1", "ps_st0")
    a_all = alloc("a_all", [P, NQH * TPC], BF16)
    wout = alloc("wout", [P, 2 * 32 * P], BF16)
    op_sb = alloc("op_sb", [P, 2 * TPC], F32)
    ps_opa = palloc("ps_opa", [P, 1024], F32)
    ps_opb = palloc("ps_opb", [P, 1024], F32)
    ps_op = (ps_opa, ps_opb)

    for kvq in range(NKVH):
        w = [cp_aspill[kvq]]
        if kvq == 0:
            w += gate()
        pl.op("sp", lambda s, kvq=kvq: s.dma_start(
            out=a_all[:, kvq * 4 * TPC:(kvq + 1) * 4 * TPC].rearrange(
                "p (o t) -> p o t", o=1),
            in_=d_attn[kvq:kvq + 1, :, :].rearrange("h p t -> p h t")),
            waits=w, inc="REL", amt=16)
    cp_rel_all = ("REL", pl.at("REL"))

    d_wout4 = d_wout.rearrange("fc dc d f -> fc d dc f")
    cp_op_mm = {}
    cp_op_ev = {}
    cp_p0 = {}
    for foc in range(32):
        par = foc % 2
        wsl = wout[:, par * 32 * P:(par + 1) * 32 * P]
        w = [("PE", cp_op_mm[foc - 2])] if foc >= 2 else gate()
        won = f"WO{foc % 2}"
        pl.op("gp", lambda g, a=wsl, foc=foc: g.dma_start(
            out=a.rearrange("p (n dc f) -> p n dc f", n=1, dc=32),
            in_=d_wout4[foc:foc + 1, :, :, :].rearrange("fc d dc f -> d fc dc f")),
              waits=w, inc=won, amt=16)
        wd = (won, pl.at(won))
        pp = ps_op[par]
        w = [wd, cp_rel_all] + (gate() if foc == 0 else [])
        if foc >= 2:
            w.append(("VE", cp_op_ev[foc - 2]))

        def op_fn(t, wsl=wsl, pp=pp):
            last = None
            for dc in range(32):
                for tg in range(2):
                    last = t.matmul(
                        pp[:, tg * 512:(tg + 1) * 512],
                        wsl[:, dc * P:(dc + 1) * P],
                        a_all[:, dc * TPC + tg * 512: dc * TPC + (tg + 1) * 512],
                        start=(dc == 0), stop=(dc == 31))
            return last
        pl.op("pe", op_fn, waits=w, inc="PE")
        cp_op_mm[foc] = pl.at("PE")
        osl = op_sb[:, par * TPC:(par + 1) * TPC]
        w = [("PE", cp_op_mm[foc])]
        if foc >= 2:
            w.append(cp_p0[foc - 2])
        pl.op("ve", lambda vv, osl=osl, pp=pp: vv.tensor_copy(osl, pp[:]),
              waits=w, inc="VE")
        cp_op_ev[foc] = pl.at("VE")
        p0n = f"P0{foc % 2}"
        pl.op("sp", lambda s, osl=osl, foc=foc: s.dma_start(
            out=d_p0[foc:foc + 1, :, :].rearrange("f p t -> p f t"),
            in_=osl.rearrange("p (o t) -> p o t", o=1)),
              waits=[("VE", cp_op_ev[foc])], inc=p0n, amt=16)
        cp_p0[foc] = (p0n, pl.at(p0n))
        if debug and foc == 0:
            pl.op("sp", lambda s, osl=osl: s.dma_start(out=d_dbg_p0[:], in_=osl),
                  waits=[("VE", cp_op_ev[0])], inc="LDBG", amt=16)
    endphase()

    free("op_sb", "wout", "a_all", "ps_opb", "ps_opa")
    xn_fm2 = alloc("xn_fm2", [P, ECH * TPC], BF16)
    h_sb = alloc("h_sb", [P, THMAX * TPC], BF16)
    wg_sb = alloc("wg_sb", [P, 2 * ECH * P], BF16)
    wu_sb = alloc("wu_sb", [P, 2 * ECH * P], BF16)
    gs_sb = alloc("gs_sb", [P, 2 * TPC], BF16)

    pl.op("sp", lambda s: s.dma_start(out=xn_fm2[:], in_=d_xn[:]),
          waits=gate() + [cp_xn_spill], inc="LXR", amt=16)
    cp_xn_rel = ("LXR", pl.at("LXR"))
    xnv2 = xn_fm2[:].rearrange("p (ec t) -> p ec t", ec=ECH)

    d_wg4 = d_wg.rearrange("n ec e f -> n e ec f")
    d_wu4 = d_wu.rearrange("n ec e f -> n e ec f")
    d_wd4 = d_wd.rearrange("n m dff fo -> n dff m fo")

    wd_sb = pstr = dn_sb = None
    ps_dn = None
    cp_pn_prev = None
    dn_ev_final_prev = None
    pn_final_prev = None
    pchain = [d_p0, d_p1, d_p2, d_p3]
    for half in range(3):
        NH = THIRDS[half]
        base = sum(THIRDS[:half])
        p_prev, p_next = pchain[half], pchain[half + 1]
        ps_g0 = palloc(f"ps_g0_{half}", [P, 1024], F32)
        ps_g1 = palloc(f"ps_g1_{half}", [P, 1024], F32)
        ps_u0 = palloc(f"ps_u0_{half}", [P, 1024], F32)
        ps_u1 = palloc(f"ps_u1_{half}", [P, 1024], F32)
        ps_g = (ps_g0, ps_g1)
        ps_u = (ps_u0, ps_u1)
        cp_gu_mm = {}
        cp_gu_ev = {}
        for dl in range(NH):
            dffc = base + dl
            par = dl % 2
            wgs = wg_sb[:, par * ECH * P:(par + 1) * ECH * P]
            wus = wu_sb[:, par * ECH * P:(par + 1) * ECH * P]
            if dl >= 2:
                w = [("PE", cp_gu_mm[dl - 2])]
            else:
                w = [("PE", snap["PE"])]
            pl.op("gp", lambda g, a=wgs, dffc=dffc: g.dma_start(
                out=a.rearrange("p (n ec f) -> p n ec f", n=1, ec=ECH),
                in_=d_wg4[dffc:dffc + 1, :, :, :].rearrange("n e ec f -> e n ec f")),
                  waits=w, inc=f"WG{par}", amt=16)
            pl.op("gp", lambda g, a=wus, dffc=dffc: g.dma_start(
                out=a.rearrange("p (n ec f) -> p n ec f", n=1, ec=ECH),
                in_=d_wu4[dffc:dffc + 1, :, :, :].rearrange("n e ec f -> e n ec f")),
                  inc=f"WU{par}", amt=16)
            wd2 = [(f"WG{par}", pl.at(f"WG{par}")), (f"WU{par}", pl.at(f"WU{par}"))]
            gp_ = ps_g[par]
            up_ = ps_u[par]
            w = list(wd2)
            if dl == 0:
                w.append(cp_xn_rel)
                w += gate()
            if dl >= 2:
                w.append(("VE", cp_gu_ev[dl - 2]))

            def gu_fn(t, wgs=wgs, wus=wus, gp_=gp_, up_=up_):
                last = None
                for ec in range(ECH):
                    for tg in range(2):
                        t.matmul(gp_[:, tg * 512:(tg + 1) * 512],
                                 wgs[:, ec * P:(ec + 1) * P],
                                 xnv2[:, ec, tg * 512:(tg + 1) * 512],
                                 start=(ec == 0), stop=(ec == ECH - 1))
                        last = t.matmul(up_[:, tg * 512:(tg + 1) * 512],
                                        wus[:, ec * P:(ec + 1) * P],
                                        xnv2[:, ec, tg * 512:(tg + 1) * 512],
                                        start=(ec == 0), stop=(ec == ECH - 1))
                return last
            pl.op("pe", gu_fn, waits=w, inc="PE")
            cp_gu_mm[dl] = pl.at("PE")
            gsl = gs_sb[:, par * TPC:(par + 1) * TPC]
            pl.op("sc", lambda s, gsl=gsl, gp_=gp_, dffc=dffc: s.activation(
                out=gsl, in_=gp_[:], func=AF.Silu,
                scale=c_gsc[:, dffc:dffc + 1]),
                  waits=[("PE", cp_gu_mm[dl])], inc="SC")
            sc_si = pl.at("SC")
            pl.op("ve", lambda vv, dl=dl, up_=up_, gsl=gsl, dffc=dffc:
                  vv.scalar_tensor_tensor(
                      h_sb[:, dl * TPC:(dl + 1) * TPC], up_[:],
                      c_usc[:, dffc:dffc + 1], gsl,
                      AluOpType.mult, AluOpType.mult),
                  waits=[("SC", sc_si)], inc="VE")
            cp_gu_ev[dl] = pl.at("VE")
            if debug and half == 0 and dl == 0:
                pl.op("sp", lambda s: s.dma_start(out=d_dbg_h[:], in_=h_sb[:, 0:TPC]),
                      waits=[("VE", cp_gu_ev[0])], inc="LDBG", amt=16)
        cp_gu_all = pl.at("VE")
        endphase()

        free(f"ps_u1_{half}", f"ps_u0_{half}", f"ps_g1_{half}", f"ps_g0_{half}")
        if half == 0:
            wd_sb = alloc("wd_sb", [P, 2 * THMAX * P], BF16)
            pstr = alloc("pstr", [P, 2 * TPC], F32)
            dn_sb = alloc("dn_sb", [P, 2 * TPC], F32)
        ps_dn0 = palloc(f"ps_dn0_{half}", [P, 1024], F32)
        ps_dn1 = palloc(f"ps_dn1_{half}", [P, 1024], F32)
        ps_dn = (ps_dn0, ps_dn1)
        cp_dn_mm = {}
        cp_dn_ev = {}
        cp_pn = {}
        cp_pstr = {}
        for foc in range(32):
            par = foc % 2
            wds = wd_sb[:, par * THMAX * P:(par + 1) * THMAX * P]
            if foc >= 2:
                w = [("PE", cp_dn_mm[foc - 2])]
            else:
                w = [("PE", snap["PE"])] + (gate() if half == 0 else [])
            pl.op("gp", lambda g, a=wds[:, 0:NH * P], foc=foc, NH=NH, base=base:
                  g.dma_start(
                      out=a.rearrange("p (n m f) -> p n m f", n=1, m=NH),
                      in_=d_wd4[foc:foc + 1, :, base:base + NH, :].rearrange(
                          "n dff m fo -> dff n m fo")),
                  waits=w, inc=f"WN{par}", amt=16)
            wd = (f"WN{par}", pl.at(f"WN{par}"))
            psl = pstr[:, par * TPC:(par + 1) * TPC]
            w = [cp_p0[foc] if half == 0 else cp_pn_prev[foc]]
            if foc >= 2:
                w.append(("VE", cp_dn_ev[foc - 2]))
            elif half == 0:
                w += gate()
            else:
                w.append(("VE", dn_ev_final_prev))
            pl.op("sp", lambda s, psl=psl, foc=foc, src=p_prev: s.dma_start(
                out=psl.rearrange("p (o t) -> p o t", o=1),
                in_=src[foc:foc + 1, :, :].rearrange("f p t -> p f t")),
                  waits=w, inc=f"PS{par}", amt=16)
            cp_pstr[foc] = (f"PS{par}", pl.at(f"PS{par}"))
            pp = ps_dn[par]
            w = [wd, ("VE", cp_gu_all)]
            if foc >= 2:
                w.append(("VE", cp_dn_ev[foc - 2]))

            def dn_fn(t, wds=wds, pp=pp, NH=NH):
                last = None
                for dl in range(NH):
                    for tg in range(2):
                        last = t.matmul(
                            pp[:, tg * 512:(tg + 1) * 512],
                            wds[:, dl * P:(dl + 1) * P],
                            h_sb[:, dl * TPC + tg * 512: dl * TPC + (tg + 1) * 512],
                            start=(dl == 0), stop=(dl == NH - 1))
                return last
            pl.op("pe", dn_fn, waits=w, inc="PE")
            cp_dn_mm[foc] = pl.at("PE")
            dsl = dn_sb[:, par * TPC:(par + 1) * TPC]
            w = [("PE", cp_dn_mm[foc]), cp_pstr[foc]]
            if foc >= 2:
                w.append(cp_pn[foc - 2])
            elif half >= 1:
                w += pn_final_prev
            pl.op("ve", lambda vv, dsl=dsl, pp=pp, psl=psl, foc=foc:
                  vv.scalar_tensor_tensor(dsl, pp[:], c_dsc[:, foc:foc + 1],
                                          psl, AluOpType.mult, AluOpType.add),
                  waits=w, inc="VE")
            cp_dn_ev[foc] = pl.at("VE")
            pnn = f"PN{half}{par}"
            pl.op("sp", lambda s, dsl=dsl, foc=foc, dst=p_next: s.dma_start(
                out=dst[foc:foc + 1, :, :].rearrange("f p t -> p f t"),
                in_=dsl.rearrange("p (o t) -> p o t", o=1)),
                  waits=[("VE", cp_dn_ev[foc])], inc=pnn, amt=16)
            cp_pn[foc] = (pnn, pl.at(pnn))
        cp_pn_prev = dict(cp_pn)
        dn_ev_final_prev = pl.at("VE")
        pn_final_prev = [(f"PN{half}0", pl.at(f"PN{half}0")),
                         (f"PN{half}1", pl.at(f"PN{half}1"))]
        free(f"ps_dn1_{half}", f"ps_dn0_{half}")
        endphase()

    free("dn_sb", "pstr", "wd_sb", "gs_sb", "wu_sb", "wg_sb", "h_sb", "xn_fm2")
    f_sb = alloc("f_sb", [P, 2 * E], F32)
    xt2 = alloc("xt2", [P, 2 * E], F32)
    o_sb = alloc("o_sb", [P, 2 * E], F32)
    ps_fta = palloc("ps_fta", [P, 512], F32)
    ps_ftb = palloc("ps_ftb", [P, 512], F32)
    ps_ft = (ps_fta, ps_ftb)

    cp_ft = {}
    cp_oev = {}
    cp_st = {}
    p2_all = [(f"PN20", pl.at("PN20")), (f"PN21", pl.at("PN21"))]
    ft_free = [pl.at("VE"), pl.at("VE")]
    fgi = 0
    for tc in range(NCH):
        par = tc % 2
        fsl = f_sb[:, par * E:(par + 1) * E]
        xsl = xt2[:, par * E:(par + 1) * E]
        osl = o_sb[:, par * E:(par + 1) * E]
        w = list(p2_all) + (gate() if tc == 0 else [])
        if tc >= 2:
            w.append(("PE", cp_ft[(tc - 2, 7)]))
        fv = fsl.rearrange("p (foc t) -> p foc t", foc=32)
        fln = f"FL{tc % 2}"
        for q in range(4):
            pl.op("sp", lambda s, fv=fv, q=q, tc=tc: s.dma_start(
                out=fv[:, 8 * q:8 * q + 8, :],
                in_=d_p3[8 * q:8 * q + 8, :, tc * P:(tc + 1) * P].rearrange(
                    "foc p t -> p foc t")),
                  waits=w if q == 0 else (), inc=fln, amt=16)
        cp_f = (fln, pl.at(fln))
        w = [("VE", cp_oev[(tc - 2, 7)])] if tc >= 2 else (gate() if tc == 0 else [])
        pl.op("sp", lambda s, xsl=xsl, tc=tc: s.dma_start(
            out=xsl, in_=d_x[tc * P:(tc + 1) * P, :]), waits=w,
              inc=f"X2{tc % 2}", amt=16)
        cp_ldx2 = (f"X2{tc % 2}", pl.at(f"X2{tc % 2}"))
        for g in range(8):
            gpar = fgi % 2
            fgi += 1
            fp = ps_ft[gpar]
            w = [cp_f, ("VE", ft_free[gpar])]

            def ft_fn(t, fsl=fsl, fp=fp, g=g):
                last = None
                for i2 in range(4):
                    foc = 4 * g + i2
                    last = t.transpose(fp[:, i2 * P:(i2 + 1) * P],
                                       fsl[:, foc * P:(foc + 1) * P], c_idf[:])
                return last
            pl.op("pe", ft_fn, waits=w, inc="PE")
            cp_ft[(tc, g)] = pl.at("PE")
            w = [("PE", cp_ft[(tc, g)]), cp_ldx2]
            if g == 0 and tc >= 2:
                w.append(cp_st[tc - 2])
            pl.op("ve", lambda vv, osl=osl, xsl=xsl, fp=fp, g=g: vv.tensor_add(
                osl[:, 512 * g:512 * (g + 1)], fp[:], xsl[:, 512 * g:512 * (g + 1)]),
                  waits=w, inc="VE")
            cp_oev[(tc, g)] = pl.at("VE")
            ft_free[gpar] = cp_oev[(tc, g)]
        stn = f"ST{tc % 2}"
        pl.op("sp", lambda s, osl=osl, tc=tc: s.dma_start(
            out=d_out[tc * P:(tc + 1) * P, :], in_=osl),
              waits=[("VE", cp_oev[(tc, 7)])], inc=stn, amt=16)
        cp_st[tc] = (stn, pl.at(stn))

    pl.op("sp", lambda s: s.nop(),
          waits=[("ST0", pl.at("ST0")), ("ST1", pl.at("ST1"))])

    sems = {}
    sem_cms = []
    for name in sems_all:
        cm = nc.semaphore(name)
        sems[name] = cm.__enter__()
        sem_cms.append(cm)

    blk_cm = nc.Block()
    block = blk_cm.__enter__()

    def runner(engname):
        def run(eng):
            for waits, fn, inc, amt, drain in pl.ops[engname]:
                for s, vv in waits:
                    eng.wait_ge(sems[s], vv)
                inst = fn(eng)
                if inc is not None:
                    inst.then_inc(sems[inc], amt)
                if drain:
                    eng.drain()
        return run

    block.tensor(runner("pe"))
    block.vector(runner("ve"))
    block.scalar(runner("sc"))
    block.sync(runner("sp"))
    block.gpsimd(runner("gp"))

    blk_cm.__exit__(None, None, None)
    for cm in sem_cms:
        cm.__exit__(None, None, None)
    for cm in reversed(list(open_cms.values())):
        cm.__exit__(None, None, None)
    return nc


def _host_prep(L0_attn_x, L0_attn_w_in, L0_attn_w_out, L0_ffn_gate_w,
               L0_ffn_gate_s, L0_ffn_up_w, L0_ffn_up_s, L0_ffn_down_w,
               L0_ffn_down_s):
    bf = ml_dtypes.bfloat16
    x = np.ascontiguousarray(np.asarray(L0_attn_x, dtype=np.float32))
    w_in = np.asarray(L0_attn_w_in, dtype=np.float32)
    w_out = np.asarray(L0_attn_w_out, dtype=np.float32)
    gw = np.asarray(L0_ffn_gate_w, dtype=np.int32)
    uw = np.asarray(L0_ffn_up_w, dtype=np.int32)
    dw = np.asarray(L0_ffn_down_w, dtype=np.int32)

    w_in_t = np.ascontiguousarray(w_in.reshape(FCQKV, P, ECH, P).transpose(0, 2, 3, 1))
    w_out_t = np.ascontiguousarray(w_out.reshape(32, P, 32, P).transpose(0, 2, 3, 1))
    gate_t = np.ascontiguousarray(gw.reshape(DFFC, P, ECH, P).transpose(0, 2, 3, 1))
    up_t = np.ascontiguousarray(uw.reshape(DFFC, P, ECH, P).transpose(0, 2, 3, 1))
    down_t = np.ascontiguousarray(dw.reshape(32, P, DFFC, P).transpose(0, 2, 3, 1))
    gsc = np.ascontiguousarray(np.asarray(L0_ffn_gate_s, np.float32).reshape(DFFC, P).T)
    usc = np.ascontiguousarray(np.asarray(L0_ffn_up_s, np.float32).reshape(DFFC, P).T)
    dsc = np.ascontiguousarray(np.asarray(L0_ffn_down_s, np.float32).reshape(32, P).T)
    identb = np.eye(P, dtype=np.float32).astype(bf)
    identf = np.eye(P, dtype=np.float32)

    inv_freq = (1.0 / 10000.0 ** (np.arange(0, HD, 2, dtype=np.float32) / HD))

    in_maps = []
    row_idx = []
    for c in range(8):
        b, j = c // 4, c % 4
        rows = np.concatenate([np.arange(P) + (4 * v + j) * P for v in range(NCH)])
        row_idx.append((b, rows))
        xc = np.ascontiguousarray(x[b][rows])
        tglob = rows.astype(np.float32)
        ang = tglob[None, :] * inv_freq[:, None]
        rs = float(SCALE ** 0.5)
        cos = np.cos(ang) * rs
        sin = np.sin(ang) * rs
        costab = np.ascontiguousarray(np.concatenate([cos, cos], axis=0).astype(bf))
        sintab = np.ascontiguousarray(np.concatenate([-sin, sin], axis=0).astype(bf))
        maskt = np.zeros((P, 512), dtype=np.float32)
        ppg, kkg = np.meshgrid(np.arange(P), np.arange(512), indexing="ij")
        maskt[kkg > (128 * j + ppg)] = NEG
        masktt = np.zeros((P, 2048), dtype=np.float32)
        kk1, qq1 = np.meshgrid(np.arange(P), np.arange(P), indexing="ij")
        for kc_rel in range(4):
            if kc_rel < j:
                m = np.zeros((P, P), dtype=np.float32)
            elif kc_rel == j:
                m = np.where(kk1 <= qq1, 0.0, NEG).astype(np.float32)
            else:
                m = np.full((P, P), NEG, dtype=np.float32)
            for hp in range(4):
                masktt[:, kc_rel * 512 + hp * P:kc_rel * 512 + (hp + 1) * P] = m
        in_maps.append(dict(
            x=xc, w_in_t=w_in_t, w_out_t=w_out_t, gate_t=gate_t, up_t=up_t,
            down_t=down_t, gsc=gsc, usc=usc, dsc=dsc, costab=costab,
            sintab=sintab, maskt=maskt, identb=identb, identf=identf,
            negb=np.full((P, 1), -40.0, dtype=np.float32),
            onest=np.ones((P, P), dtype=np.float32).astype(bf),
            masktt=masktt))
    return in_maps, row_idx


_NC_CACHE = {}


def _run(inputs, debug=False):
    in_maps, row_idx = _host_prep(**inputs)
    key = ("nc", debug)
    if key not in _NC_CACHE:
        _NC_CACHE[key] = build_nc(debug=debug)
    nc = _NC_CACHE[key]
    res = run_bass_kernel_spmd(nc, in_maps, core_ids=list(range(8)))
    out = np.empty((2, 4096, 4096), dtype=np.float32)
    for c in range(8):
        b, rows = row_idx[c]
        out[b][rows] = np.asarray(res.results[c]["out"])
    return out, res


def kernel(**inputs):
    return _run(inputs, debug=False)[0]


if __name__ == "__main__":
    import reference as ref
    ins = ref.setup_inputs(0)
    ins = {k: np.asarray(v) for k, v in ins.items()}
    got = kernel(**ins)
    want = np.asarray(ref.reference(**ins))
    num = float(np.mean((got - want) ** 2))
    den = float(np.mean(want ** 2))
    print("resid var ratio:", num / den)
